# version 08
# baseline (speedup 1.0000x reference)
import contextlib
import numpy as np
import ml_dtypes
import concourse.bass as bass
import concourse.mybir as mybir
from concourse.bass_utils import run_bass_kernel_spmd

F32 = mybir.dt.float32
BF16 = mybir.dt.bfloat16
I32 = mybir.dt.int32
I16 = mybir.dt.int16
AF = mybir.ActivationFunctionType
ALU = mybir.AluOpType
AX = mybir.AxisListType

N_CORES = 8
S_TOK = 1024
NT = 8
D = 4096
DK = 32
INTER = 11008
TI = 86
PACK_GU = 2048
PACK_D = 5504
HT = 32
NHC = 8
QKV_SCALE = float(1.0 / np.sqrt(128.0))
EPS = 1e-5


def build_nc(debug=False, serial=False):
    nc = bass.Bass("TRN2", target_bir_lowering=False)

    d_x = nc.dram_tensor("x", [S_TOK, D], F32, kind="ExternalInput")
    d_wq = nc.dram_tensor("wq_t", [128, DK * 4096], F32, kind="ExternalInput")
    d_wk = nc.dram_tensor("wk_t", [128, DK * 128], F32, kind="ExternalInput")
    d_wv = nc.dram_tensor("wv_t", [128, DK * 128], F32, kind="ExternalInput")
    d_bq = nc.dram_tensor("bq", [128, 32], F32, kind="ExternalInput")
    d_bk = nc.dram_tensor("bk", [128, 1], F32, kind="ExternalInput")
    d_bv = nc.dram_tensor("bv", [128, 1], F32, kind="ExternalInput")
    d_gp = nc.dram_tensor("gate_p", [128, TI * PACK_GU], I32, kind="ExternalInput")
    d_up = nc.dram_tensor("up_p", [128, TI * PACK_GU], I32, kind="ExternalInput")
    d_dp = nc.dram_tensor("down_p", [128, HT * PACK_D], I32, kind="ExternalInput")
    d_scg = nc.dram_tensor("sc_g", [128, TI * 64], F32, kind="ExternalInput")
    d_zrg = nc.dram_tensor("zr_g", [128, TI * 64], F32, kind="ExternalInput")
    d_scu = nc.dram_tensor("sc_u", [128, TI * 64], F32, kind="ExternalInput")
    d_zru = nc.dram_tensor("zr_u", [128, TI * 64], F32, kind="ExternalInput")
    d_scd = nc.dram_tensor("sc_d", [128, HT * 172], F32, kind="ExternalInput")
    d_zrd = nc.dram_tensor("zr_d", [128, HT * 172], F32, kind="ExternalInput")
    d_id = nc.dram_tensor("ident", [128, 128], BF16, kind="ExternalInput")
    d_ones = nc.dram_tensor("onesr", [1, 128], BF16, kind="ExternalInput")
    d_bigo = nc.dram_tensor("bigo", [128, 256], BF16, kind="ExternalInput")

    d_y = nc.dram_tensor("y", [S_TOK, D], F32, kind="ExternalOutput")
    ik = "ExternalOutput" if debug else "Internal"
    d_x1 = nc.dram_tensor("x1d", [S_TOK, D], F32, kind=ik)
    d_act = nc.dram_tensor("actd", [INTER, S_TOK], BF16, kind=ik)
    d_wdt = nc.dram_tensor("wdtd", [128, HT * TI * 128], BF16, kind=ik)
    d_dbg = nc.dram_tensor("dbgd", [NT * 128, 448], F32, kind=ik) if debug         else None
    d_agd = nc.dram_tensor("agd", [172, S_TOK], BF16, kind=ik) if debug else None

    sb_id = nc.alloc_sbuf_tensor("sb_id", [128, 128], BF16)
    sb_ones = nc.alloc_sbuf_tensor("sb_ones", [1, 128], BF16)
    sb_bigo = nc.alloc_sbuf_tensor("sb_bigo", [128, 256], BF16)
    sb_x2t = nc.alloc_sbuf_tensor("sb_x2t", [128, DK * S_TOK], BF16)
    sb_xgT = nc.alloc_sbuf_tensor("sb_xgT", [128, S_TOK], BF16)
    sb_eps = nc.alloc_sbuf_tensor("sb_eps", [128, 1], F32)

    psum = nc.alloc_psum_tensor("psum", [128, 4096], F32)

    sems = {}
    cnt = {}
    for name in ("PE", "VE", "AC"):
        sems[name] = nc.alloc_semaphore("sem_" + name)
        cnt[name] = 0

    def inc(instr, name, k=1):
        instr.then_inc(sems[name], k)
        cnt[name] += k

    def incd(eng, name, k=1):
        eng.drain().then_inc(sems[name], k)
        cnt[name] += k

    def wait(eng, name, val):
        eng.wait_ge(sems[name], val)

    class DmaStream:

        def __init__(self, name, k=4):
            self.sems = [nc.alloc_semaphore(f"q_{name}{i}") for i in range(k)]
            self.cnt = [0] * k
            self.i = 0

        def issue(self, instr, gslot=None):
            r = (self.i if gslot is None else gslot) % len(self.sems)
            instr.then_inc(self.sems[r], 16)
            self.cnt[r] += 16
            if gslot is None:
                self.i += 1
            return (r, self.cnt[r])

        def wait(self, eng, tok):
            eng.wait_ge(self.sems[tok[0]], tok[1])

        def wait_all(self, eng):
            for s, c in zip(self.sems, self.cnt):
                if c:
                    eng.wait_ge(s, c)

    sax = DmaStream("ax", 4)
    swq = DmaStream("wq", 4)
    sx = DmaStream("x", 3)
    sx1s = DmaStream("x1s", 3)
    sgp = DmaStream("gp", 4)
    sas = DmaStream("as", 4)
    sdp = DmaStream("dp", 4)
    sws = DmaStream("ws", 4)
    sal = DmaStream("al", 4)
    swl = DmaStream("wl", 4)
    sxl = DmaStream("xl", 4)
    sy = DmaStream("y", 4)
    sdbg = DmaStream("dbg", 2)
    ALL_STREAMS = [sax, swq, sx, sx1s, sgp, sas, sdp, sws, sal, swl, sxl, sy,
                   sdbg]

    stack = contextlib.ExitStack()

    with nc.Block() as block:
        def _gate(fn):
            def wrapper(eng):
                for nm in ("PE", "VE", "AC"):
                    eng.wait_ge(sems[nm], cnt[nm])
                fn(eng)
            return wrapper

        _orig = {}
        for _nm in ("sync", "vector", "scalar", "tensor", "gpsimd"):
            _orig[_nm] = getattr(block, _nm)

        class _BlockProxy:
            pass

        if serial:
            bp = _BlockProxy()
            for _nm in ("vector", "scalar", "tensor"):
                setattr(bp, _nm, (lambda f, _o=_orig[_nm]: _o(_gate(f))))
            for _nm in ("sync", "gpsimd"):
                setattr(bp, _nm, _orig[_nm])
            block = bp
        pa = contextlib.ExitStack()
        sb_wqkv = pa.enter_context(nc.sbuf_tensor("sb_wqkv", [128, DK * 384], BF16))
        sb_brow = pa.enter_context(nc.sbuf_tensor("sb_brow", [1, 384], BF16))
        p0 = contextlib.ExitStack()
        sb_wqbuf = p0.enter_context(nc.sbuf_tensor("sb_wqbuf", [128, 2 * 4096], F32))
        sb_wkv = p0.enter_context(nc.sbuf_tensor("sb_wkv", [128, 2 * DK * 128], F32))
        sb_bias = p0.enter_context(nc.sbuf_tensor("sb_bias", [128, 36], F32))
        sb_b3 = p0.enter_context(nc.sbuf_tensor("sb_b3", [128, 3], BF16))
        sb_wqtmp = p0.enter_context(nc.sbuf_tensor("sb_wqtmp", [128, 128], F32))

        def p0_eps(vector):
            vector.memset(sb_eps[:], EPS)

        block.vector(p0_eps)

        wq_toks = {kt: (kt % 4, 16 * (kt // 4 + 1)) for kt in range(DK)}

        def p0_loads(sync):
            for dst, src in [(sb_id[:], d_id[:]), (sb_ones[:], d_ones[:]),
                             (sb_bigo[:], d_bigo[:]),
                             (sb_bias[:, 0:32], d_bq[:]),
                             (sb_bias[:, 32:33], d_bk[:]),
                             (sb_bias[:, 33:34], d_bv[:]),
                             (sb_wkv[:, 0:DK * 128], d_wk[:]),
                             (sb_wkv[:, DK * 128:2 * DK * 128], d_wv[:])]:
                aux_tok[0] = sax.issue(sync.dma_start(dst, src), gslot=0)
            for kt in range(2):
                t_ = swq.issue(
                    sync.dma_start(sb_wqbuf[:, kt * 4096:(kt + 1) * 4096],
                                   d_wq[:, kt * 4096:(kt + 1) * 4096]))
                assert t_ == wq_toks[kt]

        aux_tok = [None]
        block.sync(p0_loads)

        def p0_wq(vector):
            sax.wait(vector, aux_tok[0])
            for kt in range(DK):
                swq.wait(vector, wq_toks[kt])
                buf = sb_wqbuf[:, (kt % 2) * 4096:(kt % 2) * 4096 + 4096]
                vector.reduce_sum(sb_wqtmp[:],
                                  buf.rearrange("p (d h) -> p d h", d=128),
                                  axis=AX.X)
                vector.drain()
                ii = vector.tensor_scalar(
                    out=sb_wqkv[:, kt * 384:kt * 384 + 128], in0=sb_wqtmp[:],
                    scalar1=QKV_SCALE, scalar2=None, op0=ALU.mult)
                incd(vector, "VE", 1)

        block.vector(p0_wq)

        def p0_wq_loads(sync):
            for kt in range(2, DK):
                wait(sync, "VE", kt - 1)
                t_ = swq.issue(sync.dma_start(
                    sb_wqbuf[:, (kt % 2) * 4096:(kt % 2) * 4096 + 4096],
                    d_wq[:, kt * 4096:(kt + 1) * 4096]))
                assert t_ == wq_toks[kt]

        block.sync(p0_wq_loads)

        def p0_misc(vector):
            for kt in range(DK):
                vector.tensor_copy(sb_wqkv[:, kt * 384 + 128:kt * 384 + 256],
                                   sb_wkv[:, kt * 128:(kt + 1) * 128])
                vector.tensor_copy(sb_wqkv[:, kt * 384 + 256:kt * 384 + 384],
                                   sb_wkv[:, DK * 128 + kt * 128:
                                          DK * 128 + (kt + 1) * 128])
            vector.reduce_sum(sb_bias[:, 34:35],
                              sb_bias[:, 0:32].rearrange("p (a h) -> p a h", a=1),
                              axis=AX.X)
            vector.drain()
            vector.tensor_scalar(out=sb_b3[:, 0:1], in0=sb_bias[:, 34:35],
                                 scalar1=QKV_SCALE, scalar2=None, op0=ALU.mult)
            vector.tensor_copy(sb_b3[:, 1:2], sb_bias[:, 32:33])
            ii = vector.tensor_copy(sb_b3[:, 2:3], sb_bias[:, 33:34])
            incd(vector, "VE", 1)

        block.vector(p0_misc)
        ve_p0 = cnt["VE"]

        def p0_pe_bias(tensor):
            wait(tensor, "VE", ve_p0)
            ii = None
            for c in range(3):
                ii = tensor.matmul(psum[0:1, c * 128:(c + 1) * 128],
                                   lhsT=sb_b3[:, c:c + 1], rhs=sb_id[:],
                                   start=True, stop=True)
            incd(tensor, "PE", 1)

        block.tensor(p0_pe_bias)
        pe_bias = cnt["PE"]

        def p0_ve_bias(vector):
            wait(vector, "PE", pe_bias)
            ii = vector.tensor_copy(sb_brow[0:1, :], psum[0:1, 0:384])
            incd(vector, "VE", 1)

        block.vector(p0_ve_bias)
        ve_p0b = cnt["VE"]
        pe_p0 = cnt["PE"]
        p0.close()

        sb_xbuf = pa.enter_context(nc.sbuf_tensor("sb_xbuf", [128, 2 * D], F32))
        sb_x1 = pa.enter_context(nc.sbuf_tensor("sb_x1", [128, D], F32))
        sb_x1t = pa.enter_context(nc.sbuf_tensor("sb_x1t", [128, DK * 128], BF16))
        sb_xln = pa.enter_context(nc.sbuf_tensor("sb_xln", [128, D], BF16))
        sb_scA = pa.enter_context(nc.sbuf_tensor("sb_scA", [128, 4096], F32))
        sb_scB = pa.enter_context(nc.sbuf_tensor("sb_scB", [128, 4096], F32))
        sb_qkv = pa.enter_context(nc.sbuf_tensor("sb_qkv", [128, 384], F32))
        sb_sm = pa.enter_context(nc.sbuf_tensor("sb_sm", [128, 16], F32))
        sb_M = pa.enter_context(nc.sbuf_tensor("sb_M", [128, 128], F32))
        sb_Mt1 = pa.enter_context(nc.sbuf_tensor("sb_Mt1", [128, 128], F32))
        sb_Mt2 = pa.enter_context(nc.sbuf_tensor("sb_Mt2", [128, 128], F32))
        sb_S = pa.enter_context(nc.sbuf_tensor("sb_S", [128, 128], F32))
        sb_o = pa.enter_context(nc.sbuf_tensor("sb_o", [128, 128], F32))
        sb_xg = pa.enter_context(nc.sbuf_tensor("sb_xg", [128, 64], F32))
        sb_red = pa.enter_context(nc.sbuf_tensor("sb_red", [128, 32], F32))
        sb_xgb = pa.enter_context(nc.sbuf_tensor("sb_xgb", [128, 64], BF16))

        dbg_slot = [None]
        a_prev_set = [None]
        a_x_done = []
        a_qkv_evac = []
        a_x1_store = []

        def ln_emit(src_ap, dump_ap):
            def vfn(vector):
                vector.tensor_tensor(out=dump_ap, in0=src_ap, in1=src_ap,
                                     op=ALU.mult)
                vector.reduce_sum(sb_sm[:, 0:1],
                                  src_ap.rearrange("p (a q) -> p a q", a=1),
                                  axis=AX.X)
                vector.reduce_sum(sb_red[:, 0:32],
                                  dump_ap.rearrange("p (g q) -> p g q", g=32),
                                  axis=AX.X)
                vector.drain()
                vector.reduce_sum(sb_sm[:, 1:2],
                                  sb_red[:, 0:32].rearrange(
                                      "p (a q) -> p a q", a=1), axis=AX.X)
                vector.tensor_scalar(out=sb_sm[:, 2:3], in0=sb_sm[:, 0:1],
                                     scalar1=1.0 / D, scalar2=None, op0=ALU.mult)
                vector.tensor_scalar(out=sb_sm[:, 8:9], in0=sb_sm[:, 1:2],
                                     scalar1=1.0 / D, scalar2=None, op0=ALU.mult)
                vector.drain()
                vector.tensor_tensor(out=sb_sm[:, 9:10], in0=sb_sm[:, 2:3],
                                     in1=sb_sm[:, 2:3], op=ALU.mult)
                vector.drain()
                vector.tensor_tensor(out=sb_sm[:, 3:4], in0=sb_sm[:, 8:9],
                                     in1=sb_sm[:, 9:10], op=ALU.subtract)
                vector.drain()
                ii = vector.tensor_scalar(out=sb_sm[:, 10:11],
                                          in0=sb_sm[:, 3:4], scalar1=0.0,
                                          scalar2=None, op0=ALU.max)
                incd(vector, "VE", 1)

            block.vector(vfn)
            vm = cnt["VE"]

            if debug and dbg_slot[0] is not None:
                st_, = dbg_slot

                def _dump_ln(sync, st_=st_, vm=vm):
                    wait(sync, "VE", vm)
                    sdbg.issue(sync.dma_start(
                        d_dbg[st_ * 128:(st_ + 1) * 128, 400:416], sb_sm[:]))
                    sdbg.issue(sync.dma_start(
                        d_dbg[st_ * 128:(st_ + 1) * 128, 416:448],
                        sb_xbuf[:, (st_ % 2) * D:(st_ % 2) * D + 32]))

                block.sync(_dump_ln)
                dbg_slot[0] = None

            def sfn(scalar):
                wait(scalar, "VE", vm)
                scalar.activation(sb_sm[:, 4:5], sb_sm[:, 10:11], AF.Ln, bias=sb_eps[:])
                ii = scalar.activation(sb_sm[:, 5:6], sb_sm[:, 4:5], AF.Exp,
                                       scale=-0.5)
                incd(scalar, "AC", 1)

            block.scalar(sfn)
            return cnt["AC"]

        for st in range(NT):
            x_tok = [None]
            xb = sb_xbuf[:, (st % 2) * D:(st % 2) * D + D]

            def a_load(sync, st=st, x_tok=x_tok, xb=xb):
                if st == 0:
                    wait(sync, "VE", ve_p0b)
                    wait(sync, "PE", pe_p0)
                if st >= 2:
                    wait(sync, "VE", a_x_done[st - 2])
                x_tok[0] = sx.issue(
                    sync.dma_start(xb, d_x[st * 128:(st + 1) * 128, :]))

            block.sync(a_load)

            def a_gate(vector, st=st, x_tok=x_tok):
                sx.wait(vector, x_tok[0])
                if st == 0:
                    wait(vector, "PE", pe_p0)
                else:
                    sx1s.wait(vector, a_x1_store[st - 1])

            block.vector(a_gate)

            dbg_slot[0] = st
            ac_rstd1 = ln_emit(xb, sb_scB[:])

            def a_apply1(vector, ac=ac_rstd1, xb=xb):
                wait(vector, "AC", ac)
                ii = vector.tensor_scalar(out=sb_xln[:], in0=xb,
                                          scalar1=sb_sm[:, 2:3],
                                          scalar2=sb_sm[:, 5:6],
                                          op0=ALU.subtract, op1=ALU.mult)
                incd(vector, "VE", 1)

            block.vector(a_apply1)
            ve_xln1 = cnt["VE"]

            a_tr_evac = []
            for w in range(4):
                pcol = 512 + (w % 2) * 1024

                def a_tr(tensor, w=w, pcol=pcol, ve_xln1=ve_xln1, st=st):
                    if w == 0:
                        wait(tensor, "VE", ve_xln1)
                    if w < 2 and a_prev_set[0] is not None:
                        nm_, mk_ = a_prev_set[0][w + 2]
                        wait(tensor, nm_, mk_)
                    if w >= 2:
                        nm_, mk_ = a_tr_evac[w - 2]
                        wait(tensor, nm_, mk_)
                    ii = None
                    for j in range(8):
                        kt = w * 8 + j
                        ii = tensor.matmul(
                            psum[:, pcol + j * 128:pcol + (j + 1) * 128],
                            lhsT=sb_xln[:, kt * 128:(kt + 1) * 128],
                            rhs=sb_id[:], start=True, stop=True)
                    inc(ii, "PE", 1)

                block.tensor(a_tr)
                pm = cnt["PE"]

                if w % 2 == 0:
                    def a_tr_ev(vector, w=w, pcol=pcol, pm=pm):
                        wait(vector, "PE", pm)
                        vector.tensor_copy(sb_x1t[:, w * 1024:(w + 1) * 1024],
                                           psum[:, pcol:pcol + 1024])
                        incd(vector, "VE", 1)

                    block.vector(a_tr_ev)
                    a_tr_evac.append(("VE", cnt["VE"]))
                else:
                    def a_tr_ev(scalar, w=w, pcol=pcol, pm=pm):
                        wait(scalar, "PE", pm)
                        scalar.copy(sb_x1t[:, w * 1024:(w + 1) * 1024],
                                    psum[:, pcol:pcol + 1024])
                        incd(scalar, "AC", 1)

                    block.scalar(a_tr_ev)
                    a_tr_evac.append(("AC", cnt["AC"]))

            a_prev_set[0] = a_tr_evac
            ve_x1t = max(mk for nm, mk in a_tr_evac if nm == "VE")
            ac_x1t = max(mk for nm, mk in a_tr_evac if nm == "AC")

            def a_qkv(tensor, ve_x1t=ve_x1t, ac_x1t=ac_x1t, st=st):
                wait(tensor, "VE", ve_x1t)
                wait(tensor, "AC", ac_x1t)
                if st > 0:
                    wait(tensor, "AC", a_qkv_evac[st - 1])
                for kt in range(DK):
                    tensor.matmul(psum[:, 0:384],
                                  lhsT=sb_x1t[:, kt * 128:(kt + 1) * 128],
                                  rhs=sb_wqkv[:, kt * 384:(kt + 1) * 384],
                                  start=(kt == 0), stop=False)
                ii = tensor.matmul(psum[:, 0:384], lhsT=sb_ones[0:1, :],
                                   rhs=sb_brow[0:1, :], start=False, stop=True)
                incd(tensor, "PE", 1)

            block.tensor(a_qkv)
            pe_qkv = cnt["PE"]

            def a_qkv_ev(scalar, pe_qkv=pe_qkv):
                wait(scalar, "PE", pe_qkv)
                ii = scalar.copy(sb_qkv[:], psum[:, 0:384])
                incd(scalar, "AC", 1)

            block.scalar(a_qkv_ev)
            a_qkv_evac.append(cnt["AC"])
            ac_qkv = cnt["AC"]

            def a_smprep(vector, ac_qkv=ac_qkv):
                wait(vector, "AC", ac_qkv)
                kk = sb_qkv[:, 128:256].rearrange("p (a q) -> p a q", a=1)
                vector.reduce_max(sb_sm[:, 6:7], kk, axis=AX.X)
                vector.tensor_reduce(sb_sm[:, 7:8], kk, axis=AX.X, op=ALU.min)
                vector.drain()
                vector.tensor_scalar(out=sb_Mt1[:], in0=sb_qkv[:, 0:128],
                                     scalar1=sb_sm[:, 6:7], scalar2=None,
                                     op0=ALU.mult)
                vector.tensor_scalar(out=sb_Mt2[:], in0=sb_qkv[:, 0:128],
                                     scalar1=sb_sm[:, 7:8], scalar2=None,
                                     op0=ALU.mult)
                vector.drain()
                ii = vector.tensor_tensor(out=sb_M[:], in0=sb_Mt1[:],
                                          in1=sb_Mt2[:], op=ALU.max)
                incd(vector, "VE", 1)

            block.vector(a_smprep)

            if debug:
                vm_dbg = cnt["VE"]

                def a_dbg(sync, st=st, vm_dbg=vm_dbg):
                    wait(sync, "VE", vm_dbg)
                    sdbg.issue(sync.dma_start(
                        d_dbg[st * 128:(st + 1) * 128, 0:384], sb_qkv[:]))
                    sdbg.issue(sync.dma_start(
                        d_dbg[st * 128:(st + 1) * 128, 384:400], sb_sm[:]))

                block.sync(a_dbg)

            for q in range(4):
                dsl = slice(q * 32, q * 32 + 32)

                def a_sm1(vector, dsl=dsl):
                    qs_b = sb_qkv[:, 0:128][:, dsl][:, :, None].broadcast_to(
                        [128, 32, 128])
                    k_b = sb_qkv[:, 128:256][:, None, :].broadcast_to(
                        [128, 32, 128])
                    m_b = sb_M[:, dsl][:, :, None].broadcast_to([128, 32, 128])
                    scA = sb_scA[:].rearrange("p (dd e) -> p dd e", dd=32)
                    scB = sb_scB[:].rearrange("p (dd e) -> p dd e", dd=32)
                    vector.tensor_tensor(out=scA, in0=qs_b, in1=k_b, op=ALU.mult)
                    vector.drain()
                    ii = vector.tensor_tensor(out=scB, in0=scA, in1=m_b,
                                              op=ALU.subtract)
                    incd(vector, "VE", 1)

                block.vector(a_sm1)
                vm = cnt["VE"]

                def a_sm2(scalar, vm=vm):
                    wait(scalar, "VE", vm)
                    ii = scalar.activation(sb_scA[:], sb_scB[:], AF.Exp)
                    incd(scalar, "AC", 1)

                block.scalar(a_sm2)
                am = cnt["AC"]

                def a_sm3(vector, dsl=dsl, am=am):
                    wait(vector, "AC", am)
                    scA = sb_scA[:].rearrange("p (dd e) -> p dd e", dd=32)
                    scB = sb_scB[:].rearrange("p (dd e) -> p dd e", dd=32)
                    v_b = sb_qkv[:, 256:384][:, None, :].broadcast_to(
                        [128, 32, 128])
                    vector.reduce_sum(sb_S[:, dsl], scA, axis=AX.X)
                    vector.tensor_tensor(out=scB, in0=scA, in1=v_b, op=ALU.mult)
                    vector.drain()
                    ii = vector.reduce_sum(sb_o[:, dsl], scB, axis=AX.X)
                    incd(vector, "VE", 1)

                block.vector(a_sm3)

            def a_resid(vector, xb=xb):
                vector.reciprocal(sb_Mt1[:], sb_S[:])
                vector.drain()
                vector.tensor_tensor(out=sb_Mt2[:], in0=sb_o[:], in1=sb_Mt1[:],
                                     op=ALU.mult)
                vector.drain()
                vector.tensor_tensor(
                    out=sb_x1[:].rearrange("p (dd r) -> p dd r", r=32),
                    in0=xb.rearrange("p (dd r) -> p dd r", r=32),
                    in1=sb_Mt2[:, :, None].broadcast_to([128, 128, 32]),
                    op=ALU.add)
                incd(vector, "VE", 1)

            block.vector(a_resid)
            a_x_done.append(cnt["VE"])
            ve_x1 = cnt["VE"]

            def a_x1st(sync, st=st, ve_x1=ve_x1):
                wait(sync, "VE", ve_x1)
                a_x1_store.append(sx1s.issue(
                    sync.dma_start(d_x1[st * 128:(st + 1) * 128, :], sb_x1[:])))

            block.sync(a_x1st)

            ac_rstd2 = ln_emit(sb_x1[:], sb_scA[:])

            def a_apply2(vector, ac=ac_rstd2):
                wait(vector, "AC", ac)
                x1v = sb_x1[:].rearrange("p (c two) -> p c two", two=2)
                vector.tensor_scalar(out=sb_xln[:, 0:2048],
                                     in0=x1v[:, :, 0:1],
                                     scalar1=sb_sm[:, 2:3], scalar2=sb_sm[:, 5:6],
                                     op0=ALU.subtract, op1=ALU.mult)
                vector.tensor_scalar(out=sb_xln[:, 2048:4096],
                                     in0=x1v[:, :, 1:2],
                                     scalar1=sb_sm[:, 2:3], scalar2=sb_sm[:, 5:6],
                                     op0=ALU.subtract, op1=ALU.mult)
                vector.reduce_sum(sb_xg[:],
                                  sb_xln[:, 0:2048].rearrange(
                                      "p (g c) -> p g c", g=64), axis=AX.X)
                vector.reduce_sum(sb_scB[:, 0:64],
                                  sb_xln[:, 2048:4096].rearrange(
                                      "p (g c) -> p g c", g=64), axis=AX.X)
                vector.drain()
                ii = vector.tensor_tensor(out=sb_xgb[:], in0=sb_xg[:],
                                          in1=sb_scB[:, 0:64], op=ALU.add)
                incd(vector, "VE", 1)

            block.vector(a_apply2)
            ve_xln2 = cnt["VE"]

            a_tr2_evac = []
            for w in range(4):
                pcol = 512 + (w % 2) * 1024

                def a_tr2(tensor, w=w, pcol=pcol, ve_xln2=ve_xln2):
                    if w == 0:
                        wait(tensor, "VE", ve_xln2)
                    if w < 2 and a_prev_set[0] is not None:
                        nm_, mk_ = a_prev_set[0][w + 2]
                        wait(tensor, nm_, mk_)
                    if w >= 2:
                        nm_, mk_ = a_tr2_evac[w - 2]
                        wait(tensor, nm_, mk_)
                    ii = None
                    for j in range(8):
                        kt = w * 8 + j
                        ii = tensor.matmul(
                            psum[:, pcol + j * 128:pcol + (j + 1) * 128],
                            lhsT=sb_xln[:, kt * 128:(kt + 1) * 128],
                            rhs=sb_id[:], start=True, stop=True)
                    inc(ii, "PE", 1)

                block.tensor(a_tr2)
                pm = cnt["PE"]

                if w % 2 == 0:
                    def a_tr2_ev(vector, w=w, pcol=pcol, pm=pm, st=st):
                        wait(vector, "PE", pm)
                        dst = sb_x2t[:].rearrange(
                            "p (kt s) -> p kt s", s=S_TOK)[
                            :, w * 8:(w + 1) * 8, st * 128:(st + 1) * 128]
                        vector.tensor_copy(
                            dst, psum[:, pcol:pcol + 1024].rearrange(
                                "p (kt s) -> p kt s", s=128))
                        incd(vector, "VE", 1)

                    block.vector(a_tr2_ev)
                    a_tr2_evac.append(("VE", cnt["VE"]))
                else:
                    def a_tr2_ev(scalar, w=w, pcol=pcol, pm=pm, st=st):
                        wait(scalar, "PE", pm)
                        dst = sb_x2t[:].rearrange(
                            "p (kt s) -> p kt s", s=S_TOK)[
                            :, w * 8:(w + 1) * 8, st * 128:(st + 1) * 128]
                        scalar.copy(
                            dst, psum[:, pcol:pcol + 1024].rearrange(
                                "p (kt s) -> p kt s", s=128))
                        incd(scalar, "AC", 1)

                    block.scalar(a_tr2_ev)
                    a_tr2_evac.append(("AC", cnt["AC"]))

            a_prev_set[0] = a_tr2_evac
            ve_trdone = cnt["VE"]
            ac_trdone = max(mk for nm, mk in a_tr2_evac if nm == "AC")

            def a_xg_tr(tensor, ve_trdone=ve_trdone, ac_trdone=ac_trdone):
                wait(tensor, "VE", ve_trdone)
                wait(tensor, "AC", ac_trdone)
                ii = tensor.matmul(psum[0:64, 2560:2688], lhsT=sb_xgb[:],
                                   rhs=sb_id[:], start=True, stop=True)
                incd(tensor, "PE", 1)

            block.tensor(a_xg_tr)
            pm = cnt["PE"]

            def a_xg_ev(vector, st=st, pm=pm):
                wait(vector, "PE", pm)
                vector.tensor_copy(sb_xgT[0:64, st * 128:(st + 1) * 128],
                                   psum[0:64, 2560:2688])
                ii = vector.tensor_copy(sb_xgT[64:128, st * 128:(st + 1) * 128],
                                        psum[0:64, 2560:2688])
                incd(vector, "VE", 1)

            block.vector(a_xg_ev)

        ve_A = cnt["VE"]
        pe_A = cnt["PE"]
        pa.close()

        pb = contextlib.ExitStack()
        sb_scg = pb.enter_context(nc.sbuf_tensor("sb_scg", [128, TI * 64], BF16))
        sb_scu = pb.enter_context(nc.sbuf_tensor("sb_scu", [128, TI * 64], BF16))
        sb_zsT = pb.enter_context(nc.sbuf_tensor("sb_zsT", [128, TI * 128], BF16))

        pbp = contextlib.ExitStack()
        sb_sz = pbp.enter_context(nc.sbuf_tensor("sb_sz", [128, 2 * TI * 64], F32))
        sb_zs = pbp.enter_context(nc.sbuf_tensor("sb_zs", [128, TI * 64], BF16))
        sb_zs2 = pbp.enter_context(nc.sbuf_tensor("sb_zs2", [128, TI * 64], F32))

        ac_Aend = cnt["AC"]
        for wi, (dsc, dzr, scbf, roff) in enumerate(
                [(d_scg, d_zrg, sb_scg, 0), (d_scu, d_zru, sb_scu, 64)]):
            sz_tok = [None]

            def bp_load(sync, dsc=dsc, dzr=dzr, wi=wi, sz_tok=sz_tok):
                if wi == 0:
                    wait(sync, "VE", ve_A)
                    sx1s.wait_all(sync)
                else:
                    wait(sync, "VE", bp_ve_marks[wi - 1])
                sax.issue(sync.dma_start(sb_sz[:, 0:TI * 64], dsc[:]),
                          gslot=1 + wi)
                sz_tok[0] = sax.issue(
                    sync.dma_start(sb_sz[:, TI * 64:2 * TI * 64], dzr[:]),
                    gslot=1 + wi)

            block.sync(bp_load)

            def bp_ve(vector, scbf=scbf, sz_tok=sz_tok):
                sax.wait(vector, sz_tok[0])
                vector.tensor_copy(scbf[:], sb_sz[:, 0:TI * 64])
                vector.tensor_scalar(out=sb_zs2[:],
                                     in0=sb_sz[:, TI * 64:2 * TI * 64],
                                     scalar1=7.5, scalar2=-1.0,
                                     op0=ALU.subtract, op1=ALU.mult)
                vector.drain()
                ii = vector.tensor_tensor(out=sb_zs[:], in0=sb_zs2[:],
                                          in1=sb_sz[:, 0:TI * 64], op=ALU.mult)
                incd(vector, "VE", 1)

            block.vector(bp_ve)
            bp_ve_marks = bp_ve_marks if wi else []
            bp_ve_marks.append(cnt["VE"])
            ve_zs = cnt["VE"]

            bp_evacs = []
            for w0 in range(0, TI, 8):
                nw = min(8, TI - w0)
                widx = w0 // 8
                pcol = (widx % 2) * 1024

                def bp_tr(tensor, w0=w0, nw=nw, widx=widx, pcol=pcol,
                          ve_zs=ve_zs, wi=wi):
                    if widx == 0:
                        wait(tensor, "VE", ve_zs)
                        if wi == 0:
                            wait(tensor, "AC", ac_Aend)
                    if widx >= 2:
                        wait(tensor, "VE", bp_evacs[widx - 2])
                    ii = None
                    for j in range(nw):
                        t = w0 + j
                        ii = tensor.matmul(
                            psum[0:64, pcol + j * 128:pcol + (j + 1) * 128],
                            lhsT=sb_zs[:, t * 64:(t + 1) * 64], rhs=sb_id[:],
                            start=True, stop=True)
                    incd(tensor, "PE", 1)

                block.tensor(bp_tr)
                pm = cnt["PE"]

                def bp_ev(vector, w0=w0, nw=nw, pcol=pcol, pm=pm, roff=roff):
                    wait(vector, "PE", pm)
                    ii = vector.tensor_copy(
                        sb_zsT[roff:roff + 64, w0 * 128:(w0 + nw) * 128],
                        psum[0:64, pcol:pcol + nw * 128])
                    incd(vector, "VE", 1)

                block.vector(bp_ev)
                bp_evacs.append(cnt["VE"])

        pbp.close()
        ve_bp = cnt["VE"]
        pe_bp = cnt["PE"]

        sb_pk = pb.enter_context(nc.sbuf_tensor("sb_pk", [128, 4 * PACK_GU], I16))
        sb_hi = pb.enter_context(nc.sbuf_tensor("sb_hi", [128, PACK_GU], I16))
        sb_lo = pb.enter_context(nc.sbuf_tensor("sb_lo", [128, PACK_GU], I16))
        sb_wn = pb.enter_context(nc.sbuf_tensor("sb_wn", [128, 4 * D], BF16))
        sb_wt = pb.enter_context(nc.sbuf_tensor("sb_wt", [128, 4 * D], BF16))
        sb_silu = pb.enter_context(nc.sbuf_tensor("sb_silu", [128, 1024], F32))
        sb_actb = pb.enter_context(nc.sbuf_tensor("sb_actb", [128, 2 * 1024], BF16))

        b_deq_done = []
        b_tr_done = {}
        b_act_done = []
        b_act_store = []
        b_silu_marks = []
        b_wave_marks = {}
        b_pk_toks = []

        def b_emit_load(t):
            pk_tok = [None]

            def b_load(gpsimd, t=t, pk_tok=pk_tok):
                if t == 0:
                    wait(gpsimd, "VE", ve_bp)
                    sx1s.wait_all(gpsimd)
                if t >= 2:
                    wait(gpsimd, "VE", b_deq_done[t - 2])
                o = (t % 2) * PACK_GU
                sgp.issue(gpsimd.dma_start(
                    sb_pk[:, o:o + PACK_GU],
                    d_gp[:, t * PACK_GU:(t + 1) * PACK_GU]), gslot=t)
                pk_tok[0] = sgp.issue(gpsimd.dma_start(
                    sb_pk[:, 2 * PACK_GU + o:2 * PACK_GU + o + PACK_GU],
                    d_up[:, t * PACK_GU:(t + 1) * PACK_GU]), gslot=t)

            block.gpsimd(b_load)
            b_pk_toks.append(pk_tok)

        def b_emit_deq(t):
            def b_deq(vector, t=t):
                sgp.wait(vector, b_pk_toks[t][0])
                if t >= 2:
                    wait(vector, "PE", b_tr_done[t - 2])
                wb = (t % 2) * 2 * D
                for which in range(2):
                    dsc = sb_scg if which == 0 else sb_scu
                    o = (t % 2) * PACK_GU + which * 2 * PACK_GU
                    pk = sb_pk[:, o:o + PACK_GU]
                    vector.tensor_scalar(out=sb_hi[:], in0=pk, scalar1=4,
                                         scalar2=None,
                                         op0=ALU.logical_shift_right)
                    vector.tensor_scalar(out=sb_lo[:], in0=pk, scalar1=15,
                                         scalar2=None, op0=ALU.bitwise_and)
                    s_bc = dsc[:, t * 64:(t + 1) * 64][:, :, None].broadcast_to(
                        [128, 64, 32])
                    wv = sb_wn[:, wb + which * D:wb + which * D + D]
                    vector.scalar_tensor_tensor(
                        out=wv[:, 0:2048].rearrange("p (g c) -> p g c", g=64),
                        in0=sb_hi[:].rearrange("p (g c) -> p g c", g=64),
                        scalar=-7.5, op0=ALU.add, op1=ALU.mult, in1=s_bc)
                    vector.scalar_tensor_tensor(
                        out=wv[:, 2048:4096].rearrange("p (g c) -> p g c", g=64),
                        in0=sb_lo[:].rearrange("p (g c) -> p g c", g=64),
                        scalar=-7.5, op0=ALU.add, op1=ALU.mult, in1=s_bc)
                incd(vector, "VE", 1)

            block.vector(b_deq)
            b_deq_done.append(cnt["VE"])

        def b_emit_actmul(t):
            def b_actmul(vector, t=t):
                wait(vector, "AC", b_silu_marks[t])
                if t >= 2:
                    sas.wait(vector, b_act_store[t - 2])
                ab = sb_actb[:, (t % 2) * 1024:(t % 2) * 1024 + 1024]
                vector.tensor_tensor(out=ab, in0=sb_silu[:],
                                     in1=psum[:, 1024:2048], op=ALU.mult)
                incd(vector, "VE", 1)

            block.vector(b_actmul)
            b_act_done.append(cnt["VE"])
            vm = cnt["VE"]

            def b_actst(sync, t=t, vm=vm):
                wait(sync, "VE", vm)
                ab = sb_actb[:, (t % 2) * 1024:(t % 2) * 1024 + 1024]
                b_act_store.append(sas.issue(
                    sync.dma_start(d_act[t * 128:(t + 1) * 128, :], ab)))

            block.sync(b_actst)

        def b_emit_wave(tw, w):
            pcol = 2048 + (w % 2) * 1024
            wb = (tw % 2) * 2 * D
            ve_deq = b_deq_done[tw]

            def b_tr(tensor, tw=tw, w=w, pcol=pcol, ve_deq=ve_deq, wb=wb):
                if w == 0:
                    wait(tensor, "VE", ve_deq)
                    if tw == 0:
                        wait(tensor, "AC", ac_A)
                    if tw >= 1:
                        wait(tensor, "VE", b_wave_marks[tw - 1][6])
                if w == 1 and tw >= 1:
                    wait(tensor, "VE", b_wave_marks[tw - 1][7])
                if w >= 2:
                    wait(tensor, "VE", b_wave_marks[tw][w - 2])
                ii = None
                for j in range(8):
                    g = w * 8 + j
                    which, kt = divmod(g, 32)
                    ii = tensor.matmul(
                        psum[:, pcol + j * 128:pcol + (j + 1) * 128],
                        lhsT=sb_wn[:, wb + which * D + kt * 128:
                                   wb + which * D + (kt + 1) * 128],
                        rhs=sb_id[:], start=True, stop=True)
                inc(ii, "PE", 1)

            block.tensor(b_tr)
            pm = cnt["PE"]
            wtb = (tw % 2) * 2 * D

            def b_tr_ev(vector, w=w, pcol=pcol, pm=pm, wtb=wtb):
                wait(vector, "PE", pm)
                vector.tensor_copy(sb_wt[:, wtb + w * 1024:wtb + (w + 1) * 1024],
                                   psum[:, pcol:pcol + 1024])
                incd(vector, "VE", 1)

            block.vector(b_tr_ev)
            b_wave_marks.setdefault(tw, {})[w] = cnt["VE"]
            if w == 7:
                b_tr_done[tw] = pm

        def b_emit_mm(t, with_waves_of):
            wtb = (t % 2) * 2 * D
            jobs = []
            for which in range(2):
                roff = 0 if which == 0 else 64
                for half in range(2):
                    pc = which * 1024 + half * 512
                    for kt in range(DK):
                        jobs.append((pc, which * 32 + kt, half, kt == 0, False,
                                     None))
                    jobs.append((pc, None, half, False, True, roff))
            segs = [jobs[i * 17:(i + 1) * 17] for i in range(8)]
            ve_wt = max(b_wave_marks[t].values())

            for w in range(8):
                if with_waves_of is not None:
                    b_emit_wave(with_waves_of, w)

                def b_mm_seg(tensor, t=t, w=w, seg=segs[w], ve_wt=ve_wt,
                             wtb=wtb):
                    if w == 0:
                        wait(tensor, "VE", ve_wt)
                        if t > 0:
                            wait(tensor, "VE", b_act_done[t - 1])
                    ii = None
                    for pc, sl, half, st_, sp_, roff in seg:
                        if sl is not None:
                            ii = tensor.matmul(
                                psum[:, pc:pc + 512],
                                lhsT=sb_wt[:, wtb + sl * 128:wtb + (sl + 1) * 128],
                                rhs=sb_x2t[:, (sl % 32) * S_TOK + half * 512:
                                           (sl % 32) * S_TOK + half * 512 + 512],
                                start=st_, stop=False)
                        else:
                            ii = tensor.matmul(
                                psum[:, pc:pc + 512],
                                lhsT=sb_zsT[roff:roff + 64,
                                            t * 128:(t + 1) * 128],
                                rhs=sb_xgT[roff:roff + 64,
                                           half * 512:half * 512 + 512],
                                start=False, stop=True)
                    if w == 7:
                        incd(tensor, "PE", 1)
                    else:
                        inc(ii, "PE", 1)

                block.tensor(b_mm_seg)

            pm = cnt["PE"]

            def b_silu(scalar, pm=pm):
                wait(scalar, "PE", pm)
                scalar.activation(sb_silu[:], psum[:, 0:1024], AF.Silu)
                incd(scalar, "AC", 1)

            block.scalar(b_silu)
            b_silu_marks.append(cnt["AC"])

        ac_A = cnt["AC"]
        b_emit_load(0)
        b_emit_deq(0)
        b_emit_load(1)
        b_emit_deq(1)
        for w in range(8):
            b_emit_wave(0, w)
        for t in range(TI):
            if t >= 1:
                b_emit_actmul(t - 1)
            if t + 2 < TI:
                b_emit_load(t + 2)
                b_emit_deq(t + 2)
            b_emit_mm(t, t + 1 if t + 1 < TI else None)
        b_emit_actmul(TI - 1)

        pe_B = cnt["PE"]
        ve_B = cnt["VE"]
        pb.close()

        pco = contextlib.ExitStack()
        sb_zsdT1 = pco.enter_context(
            nc.sbuf_tensor("sb_zsdT1", [128, HT * 128], BF16))
        sb_zsdT2 = pco.enter_context(
            nc.sbuf_tensor("sb_zsdT2", [44, HT * 128], BF16))
        pb2 = contextlib.ExitStack()
        sb_scd = pb2.enter_context(nc.sbuf_tensor("sb_scd", [128, HT * 172], BF16))
        pb2p = contextlib.ExitStack()
        sb_szd = pb2p.enter_context(nc.sbuf_tensor("sb_szd", [128, 2 * HT * 172], F32))
        sb_zsd = pb2p.enter_context(nc.sbuf_tensor("sb_zsd", [128, HT * 172], BF16))
        sb_zsd2 = pb2p.enter_context(nc.sbuf_tensor("sb_zsd2", [128, HT * 172], F32))

        szd_tok = [None]

        def b2p_load(sync):
            wait(sync, "VE", ve_B)
            sas.wait_all(sync)
            sax.issue(sync.dma_start(sb_szd[:, 0:HT * 172], d_scd[:]), gslot=3)
            szd_tok[0] = sax.issue(
                sync.dma_start(sb_szd[:, HT * 172:2 * HT * 172], d_zrd[:]),
                gslot=3)

        block.sync(b2p_load)

        def b2p_ve(vector):
            sax.wait(vector, szd_tok[0])
            vector.tensor_copy(sb_scd[:], sb_szd[:, 0:HT * 172])
            vector.tensor_scalar(out=sb_zsd2[:],
                                 in0=sb_szd[:, HT * 172:2 * HT * 172],
                                 scalar1=7.5, scalar2=-1.0,
                                 op0=ALU.subtract, op1=ALU.mult)
            vector.drain()
            ii = vector.tensor_tensor(out=sb_zsd[:], in0=sb_zsd2[:],
                                      in1=sb_szd[:, 0:HT * 172], op=ALU.mult)
            incd(vector, "VE", 1)

        block.vector(b2p_ve)
        ve_zsd = cnt["VE"]

        b2p_marks = []
        for ht in range(HT):
            def b2p_tr(tensor, ht=ht, ve_zsd=ve_zsd):
                if ht == 0:
                    wait(tensor, "VE", ve_zsd)
                if ht >= 2:
                    wait(tensor, "VE", b2p_marks[ht - 2])
                pcol = (ht % 2) * 512
                tensor.matmul(psum[0:128, pcol:pcol + 128],
                              lhsT=sb_zsd[:, ht * 172:ht * 172 + 128],
                              rhs=sb_id[:], start=True, stop=True)
                ii = tensor.matmul(psum[0:44, pcol + 128:pcol + 256],
                                   lhsT=sb_zsd[:, ht * 172 + 128:ht * 172 + 172],
                                   rhs=sb_id[:], start=True, stop=True)
                incd(tensor, "PE", 1)

            block.tensor(b2p_tr)
            pm = cnt["PE"]

            def b2p_ev(vector, ht=ht, pm=pm):
                wait(vector, "PE", pm)
                pcol = (ht % 2) * 512
                vector.tensor_copy(sb_zsdT1[:, ht * 128:(ht + 1) * 128],
                                   psum[0:128, pcol:pcol + 128])
                ii = vector.tensor_copy(sb_zsdT2[0:44, ht * 128:(ht + 1) * 128],
                                        psum[0:44, pcol + 128:pcol + 256])
                incd(vector, "VE", 1)

            block.vector(b2p_ev)
            b2p_marks.append(cnt["VE"])

        pb2p.close()
        ve_b2p = cnt["VE"]

        sb_pkd = pb2.enter_context(nc.sbuf_tensor("sb_pkd", [128, 2 * PACK_D], I16))
        sb_hid = pb2.enter_context(nc.sbuf_tensor("sb_hid", [128, PACK_D], I16))
        sb_lod = pb2.enter_context(nc.sbuf_tensor("sb_lod", [128, PACK_D], I16))
        sb_wnd = pb2.enter_context(nc.sbuf_tensor("sb_wnd", [128, INTER], BF16))
        sb_wtd = pb2.enter_context(nc.sbuf_tensor("sb_wtd", [128, TI * 128], BF16))

        b2_deq_done = []
        b2_tr_done = []
        b2_store_done = []

        for ht in range(HT):
            pkd_tok = [None]

            def b2_load(gpsimd, ht=ht, pkd_tok=pkd_tok):
                if ht == 0:
                    wait(gpsimd, "VE", ve_b2p)
                    sas.wait_all(gpsimd)
                if ht >= 2:
                    wait(gpsimd, "VE", b2_deq_done[ht - 2])
                o = (ht % 2) * PACK_D
                pkd_tok[0] = sdp.issue(gpsimd.dma_start(
                    sb_pkd[:, o:o + PACK_D],
                    d_dp[:, ht * PACK_D:(ht + 1) * PACK_D]))

            block.gpsimd(b2_load)

            def b2_deq(vector, ht=ht, pkd_tok=pkd_tok):
                sdp.wait(vector, pkd_tok[0])
                if ht > 0:
                    wait(vector, "PE", b2_tr_done[ht - 1])
                o = (ht % 2) * PACK_D
                pk = sb_pkd[:, o:o + PACK_D]
                vector.tensor_scalar(out=sb_hid[:], in0=pk, scalar1=4,
                                     scalar2=None, op0=ALU.logical_shift_right)
                vector.tensor_scalar(out=sb_lod[:], in0=pk, scalar1=15,
                                     scalar2=None, op0=ALU.bitwise_and)
                s_bc = sb_scd[:, ht * 172:(ht + 1) * 172][:, :, None] \
                    .broadcast_to([128, 172, 32])
                vector.scalar_tensor_tensor(
                    out=sb_wnd[:, 0:PACK_D].rearrange("p (g c) -> p g c", g=172),
                    in0=sb_hid[:].rearrange("p (g c) -> p g c", g=172),
                    scalar=-7.5, op0=ALU.add, op1=ALU.mult, in1=s_bc)
                ii = vector.scalar_tensor_tensor(
                    out=sb_wnd[:, PACK_D:INTER].rearrange(
                        "p (g c) -> p g c", g=172),
                    in0=sb_lod[:].rearrange("p (g c) -> p g c", g=172),
                    scalar=-7.5, op0=ALU.add, op1=ALU.mult, in1=s_bc)
                incd(vector, "VE", 1)

            block.vector(b2_deq)
            b2_deq_done.append(cnt["VE"])
            ve_deq = cnt["VE"]

            b2_evacs = []
            for w0 in range(0, TI, 8):
                nw = min(8, TI - w0)
                widx = w0 // 8
                pcol = (widx % 2) * 1024

                def b2_tr(tensor, ht=ht, w0=w0, nw=nw, widx=widx, pcol=pcol,
                          ve_deq=ve_deq):
                    if widx == 0:
                        wait(tensor, "VE", ve_deq)
                        if ht >= 1:
                            sws.wait(tensor, b2_store_done[ht - 1])
                    if widx >= 2:
                        nm_, mk_ = b2_evacs[widx - 2]
                        wait(tensor, nm_, mk_)
                    ii = None
                    for j in range(nw):
                        kt = w0 + j
                        ii = tensor.matmul(
                            psum[:, pcol + j * 128:pcol + (j + 1) * 128],
                            lhsT=sb_wnd[:, kt * 128:(kt + 1) * 128],
                            rhs=sb_id[:], start=True, stop=True)
                    inc(ii, "PE", 1)

                block.tensor(b2_tr)
                pm = cnt["PE"]

                if widx % 2 == 0:
                    def b2_tr_ev(vector, w0=w0, nw=nw, pcol=pcol, pm=pm):
                        wait(vector, "PE", pm)
                        vector.tensor_copy(
                            sb_wtd[:, w0 * 128:(w0 + nw) * 128],
                            psum[:, pcol:pcol + nw * 128])
                        incd(vector, "VE", 1)

                    block.vector(b2_tr_ev)
                    b2_evacs.append(("VE", cnt["VE"]))
                else:
                    def b2_tr_ev(scalar, w0=w0, nw=nw, pcol=pcol, pm=pm):
                        wait(scalar, "PE", pm)
                        scalar.copy(
                            sb_wtd[:, w0 * 128:(w0 + nw) * 128],
                            psum[:, pcol:pcol + nw * 128])
                        incd(scalar, "AC", 1)

                    block.scalar(b2_tr_ev)
                    b2_evacs.append(("AC", cnt["AC"]))

            b2_tr_done.append(cnt["PE"])
            vm = cnt["VE"]
            am2 = cnt["AC"]

            def b2_st(sync, ht=ht, vm=vm, am2=am2):
                wait(sync, "VE", vm)
                wait(sync, "AC", am2)
                b2_store_done.append(sws.issue(sync.dma_start(
                    d_wdt[:, ht * TI * 128:(ht + 1) * TI * 128], sb_wtd[:])))

            block.sync(b2_st)

        pe_B2 = cnt["PE"]
        ve_B2 = cnt["VE"]
        pb2.close()

        pc_ = contextlib.ExitStack()
        sb_actt = pc_.enter_context(nc.sbuf_tensor("sb_actt", [128, 2 * 1024], BF16))
        sb_ag1 = pc_.enter_context(nc.sbuf_tensor("sb_ag1", [128, S_TOK], BF16))
        sb_ag2 = pc_.enter_context(nc.sbuf_tensor("sb_ag2", [44, S_TOK], BF16))

        c0_pe_marks = []
        act_toks = {}
        for t in range(TI):
            def c0_load(sync, t=t):
                if t == 0:
                    sas.wait_all(sync)
                    sws.wait_all(sync)
                    sx1s.wait_all(sync)
                if t >= 2:
                    wait(sync, "PE", c0_pe_marks[t - 2])
                o = (t % 2) * 1024
                act_toks[t] = sal.issue(
                    sync.dma_start(sb_actt[:, o:o + 1024],
                                   d_act[t * 128:(t + 1) * 128, :]))

            block.sync(c0_load)

            def c0_mm(tensor, t=t):
                sal.wait(tensor, act_toks[t])
                if t == 0:
                    wait(tensor, "VE", ve_B2)
                o = (t % 2) * 1024
                if t < 32:
                    base, T, first, last = 0, t, (t == 0), False
                elif t < 43:
                    base, T, first, last = 1024, t - 32, (t == 32), False
                elif t < 75:
                    base, T, first, last = 0, t - 43, False, (t == 74)
                else:
                    base, T, first, last = 1024, t - 75, False, (t == 85)
                lhs = sb_bigo[:, 124 - 4 * T:252 - 4 * T]
                ii = None
                for half in range(2):
                    ii = tensor.matmul(
                        psum[:, base + half * 512:base + half * 512 + 512],
                        lhsT=lhs,
                        rhs=sb_actt[:, o + half * 512:o + half * 512 + 512],
                        start=first, stop=last)
                inc(ii, "PE", 1)

            block.tensor(c0_mm)
            c0_pe_marks.append(cnt["PE"])

        def c0_fence(tensor):
            incd(tensor, "PE", 1)

        block.tensor(c0_fence)
        pe_c0 = cnt["PE"]

        def c0_ev(vector):
            wait(vector, "PE", pe_c0)
            vector.tensor_copy(sb_ag1[:], psum[:, 0:1024])
            ii = vector.tensor_copy(sb_ag2[0:44, :], psum[0:44, 1024:2048])
            incd(vector, "VE", 1)

        block.vector(c0_ev)
        ve_ag = cnt["VE"]

        if debug:
            def c0_dump(sync):
                wait(sync, "VE", ve_ag)
                sdbg.issue(sync.dma_start(d_agd[0:128, :], sb_ag1[:]))
                sdbg.issue(sync.dma_start(d_agd[128:172, :], sb_ag2[0:44, :]))

            block.sync(c0_dump)

        sb_wdtb = pc_.enter_context(
            nc.sbuf_tensor("sb_wdtb", [128, 2 * 16 * 512], BF16))
        sb_x1c = pc_.enter_context(
            nc.sbuf_tensor("sb_x1c", [128, 2 * 8 * 512], F32))
        sb_y = pc_.enter_context(nc.sbuf_tensor("sb_y", [128, 8 * 512], F32))

        KT_BATCH = [list(range(b, min(b + 16, TI))) for b in range(0, TI, 16)]
        wdt_v = d_wdt[:].rearrange("p (ht kt h) -> p ht kt h", ht=HT, kt=TI)

        c_evac_marks = []
        c_x1_tok = {}
        c_y_tok = {}
        c_wdt_used = []
        c_act_used = []
        for hc in range(NHC):
            def c_x1load(sync, hc=hc):
                if hc == 0:
                    wait(sync, "VE", ve_ag)
                if hc >= 2:
                    wait(sync, "VE", c_evac_marks[hc - 2])
                o = (hc % 2) * 8 * 512
                for st2 in range(NT):
                    c_x1_tok[hc] = sxl.issue(sync.dma_start(
                        sb_x1c[:, o + st2 * 512:o + (st2 + 1) * 512],
                        d_x1[st2 * 128:(st2 + 1) * 128,
                             hc * 512:(hc + 1) * 512]), gslot=hc)

            block.sync(c_x1load)

            for bi, batch in enumerate(KT_BATCH):
                wdt_tok = [None]

                def c_wdtload(sync, hc=hc, bi=bi, batch=batch):
                    if len(c_wdt_used) >= 2:
                        wait(sync, "PE", c_wdt_used[-2])
                    o = (bi % 2) * 16 * 512
                    nb = len(batch)
                    src = wdt_v[:, hc * 4:(hc + 1) * 4,
                                batch[0]:batch[0] + nb, :]
                    dst = sb_wdtb[:, o:o + 16 * 512].rearrange(
                        "p (ht kt h) -> p ht kt h", ht=4, kt=16)[:, :, 0:nb, :]
                    wdt_tok[0] = swl.issue(sync.dma_start(dst, src))

                block.sync(c_wdtload)

                for kk, kt in enumerate(batch):
                    gidx = hc * TI + kt

                    ca_tok = [None]

                    def c_actload(sync, hc=hc, kt=kt, gidx=gidx, ca_tok=ca_tok):
                        if gidx < 2:
                            wait(sync, "PE", pe_c0)
                        else:
                            wait(sync, "PE", c_act_used[gidx - 2])
                        o = (kt % 2) * 1024
                        ca_tok[0] = sal.issue(
                            sync.dma_start(sb_actt[:, o:o + 1024],
                                           d_act[kt * 128:(kt + 1) * 128, :]))

                    block.sync(c_actload)

                    def c_mm(tensor, hc=hc, bi=bi, kk=kk, kt=kt,
                             wdt_tok=wdt_tok, ca_tok=ca_tok):
                        swl.wait(tensor, wdt_tok[0])
                        sal.wait(tensor, ca_tok[0])
                        if hc == 0 and bi == 0 and kk == 0:
                            wait(tensor, "VE", ve_ag)
                        if hc > 0 and bi == 0 and kk == 0:
                            wait(tensor, "VE", c_evac_marks[hc - 1])
                        o = (kt % 2) * 1024
                        ob = (bi % 2) * 16 * 512
                        rhs = sb_wdtb[:, ob:ob + 16 * 512].rearrange(
                            "p (ht kt h) -> p ht kt h", ht=4,
                            kt=16)[:, :, kk, :]
                        ii = None
                        for st2 in range(NT):
                            ii = tensor.matmul(
                                psum[:, st2 * 512:(st2 + 1) * 512],
                                lhsT=sb_actt[:, o + st2 * 128:
                                             o + (st2 + 1) * 128],
                                rhs=rhs,
                                start=(bi == 0 and kk == 0), stop=False)
                        inc(ii, "PE", 1)

                    block.tensor(c_mm)
                    c_act_used.append(cnt["PE"])

                c_wdt_used.append(cnt["PE"])

            def c_corr(tensor, hc=hc):
                ii = None
                for st2 in range(NT):
                    tensor.matmul(
                        psum[:, st2 * 512:(st2 + 1) * 512],
                        lhsT=sb_ag1[:, st2 * 128:(st2 + 1) * 128],
                        rhs=sb_zsdT1[:, hc * 512:(hc + 1) * 512],
                        start=False, stop=False)
                    ii = tensor.matmul(
                        psum[:, st2 * 512:(st2 + 1) * 512],
                        lhsT=sb_ag2[0:44, st2 * 128:(st2 + 1) * 128],
                        rhs=sb_zsdT2[0:44, hc * 512:(hc + 1) * 512],
                        start=False, stop=True)
                incd(tensor, "PE", 1)

            block.tensor(c_corr)
            pe_hc = cnt["PE"]

            def c_evac(vector, hc=hc, pe_hc=pe_hc):
                wait(vector, "PE", pe_hc)
                sxl.wait(vector, c_x1_tok[hc])
                if hc >= 1:
                    sy.wait(vector, c_y_tok[hc - 1])
                o = (hc % 2) * 8 * 512
                ii = None
                for st2 in range(NT):
                    ii = vector.tensor_tensor(
                        out=sb_y[:, st2 * 512:(st2 + 1) * 512],
                        in0=psum[:, st2 * 512:(st2 + 1) * 512],
                        in1=sb_x1c[:, o + st2 * 512:o + (st2 + 1) * 512],
                        op=ALU.add)
                incd(vector, "VE", 1)

            block.vector(c_evac)
            c_evac_marks.append(cnt["VE"])
            vm = cnt["VE"]

            def c_store(sync, hc=hc, vm=vm):
                wait(sync, "VE", vm)
                for st2 in range(NT):
                    c_y_tok[hc] = sy.issue(sync.dma_start(
                        d_y[st2 * 128:(st2 + 1) * 128,
                            hc * 512:(hc + 1) * 512],
                        sb_y[:, st2 * 512:(st2 + 1) * 512]), gslot=hc)

            block.sync(c_store)

        def fin(sync):
            for s_ in ALL_STREAMS:
                s_.wait_all(sync)

        block.sync(fin)
        pc_.close()
        pco.close()

    return nc


def kernel(L0_attn_x, L0_attn_Wq, L0_attn_bq, L0_attn_Wk, L0_attn_bk,
           L0_attn_Wv, L0_attn_bv, L0_ffn_gate_Wq, L0_ffn_up_Wq,
           L0_ffn_down_Wq, L0_ffn_gate_scale, L0_ffn_gate_zero,
           L0_ffn_up_scale, L0_ffn_up_zero, L0_ffn_down_scale,
           L0_ffn_down_zero):
    bf = ml_dtypes.bfloat16
    x = np.ascontiguousarray(np.asarray(L0_attn_x, dtype=np.float32)
                             .reshape(8192, 4096))
    perm = np.concatenate([np.arange(0, INTER, 2), np.arange(1, INTER, 2)])

    def tile_rows(a, ntile):
        a = np.ascontiguousarray(a)
        r, c = a.shape
        assert r == ntile * 128
        return np.ascontiguousarray(
            a.reshape(ntile, 128, c).transpose(1, 0, 2).reshape(128, ntile * c))

    gate_p = tile_rows(np.asarray(L0_ffn_gate_Wq, dtype=np.int32)[perm], TI)
    up_p = tile_rows(np.asarray(L0_ffn_up_Wq, dtype=np.int32)[perm], TI)
    down_p = tile_rows(np.asarray(L0_ffn_down_Wq, dtype=np.int32), HT)
    sc_g = tile_rows(np.asarray(L0_ffn_gate_scale, dtype=np.float32)[perm], TI)
    zr_g = tile_rows(np.asarray(L0_ffn_gate_zero, dtype=np.float32)[perm], TI)
    sc_u = tile_rows(np.asarray(L0_ffn_up_scale, dtype=np.float32)[perm], TI)
    zr_u = tile_rows(np.asarray(L0_ffn_up_zero, dtype=np.float32)[perm], TI)
    sc_d = tile_rows(np.asarray(L0_ffn_down_scale, dtype=np.float32), HT)
    zr_d = tile_rows(np.asarray(L0_ffn_down_zero, dtype=np.float32), HT)

    wq = np.asarray(L0_attn_Wq, dtype=np.float32).reshape(4096, 4096)
    wq_t = tile_rows(wq, DK)
    wk_t = tile_rows(np.asarray(L0_attn_Wk, dtype=np.float32)[:, :, 0], DK)
    wv_t = tile_rows(np.asarray(L0_attn_Wv, dtype=np.float32)[:, :, 0], DK)
    bq = np.ascontiguousarray(np.asarray(L0_attn_bq, dtype=np.float32))
    bk = np.ascontiguousarray(np.asarray(L0_attn_bk, dtype=np.float32))
    bv = np.ascontiguousarray(np.asarray(L0_attn_bv, dtype=np.float32))

    ident = np.eye(128, dtype=bf)
    onesr = np.ones((1, 128), dtype=bf)
    bigo = np.zeros((128, 256), dtype=np.float32)
    for p in range(128):
        bigo[p, 124 + p // 32] = 1.0
    bigo = np.ascontiguousarray(bigo.astype(bf))

    shared = {
        "wq_t": wq_t, "wk_t": wk_t, "wv_t": wv_t, "bq": bq, "bk": bk, "bv": bv,
        "gate_p": gate_p, "up_p": up_p, "down_p": down_p,
        "sc_g": sc_g, "zr_g": zr_g, "sc_u": sc_u, "zr_u": zr_u,
        "sc_d": sc_d, "zr_d": zr_d,
        "ident": ident, "onesr": onesr, "bigo": bigo,
    }
    in_maps = []
    for c in range(N_CORES):
        m = dict(shared)
        m["x"] = np.ascontiguousarray(x[c * S_TOK:(c + 1) * S_TOK])
        in_maps.append(m)

    nc = build_nc()
    res = run_bass_kernel_spmd(nc, in_maps, core_ids=list(range(N_CORES)))
    y = np.concatenate([np.asarray(r["y"]) for r in res.results], axis=0)
    return np.ascontiguousarray(y.reshape(1, 8192, 4096))


# version 10 (final)
# speedup vs baseline: 1.2833x; 1.2833x over previous
import contextlib
import numpy as np
import ml_dtypes
import concourse.bass as bass
import concourse.mybir as mybir
from concourse.bass_utils import run_bass_kernel_spmd

F32 = mybir.dt.float32
BF16 = mybir.dt.bfloat16
I32 = mybir.dt.int32
I16 = mybir.dt.int16
AF = mybir.ActivationFunctionType
ALU = mybir.AluOpType
AX = mybir.AxisListType

N_CORES = 8
S_TOK = 1024
NT = 8
D = 4096
DK = 32
INTER = 11008
TI = 86
PACK_GU = 2048
PACK_D = 5504
HT = 32
NHC = 8
QKV_SCALE = float(1.0 / np.sqrt(128.0))
EPS = 1e-5


def build_nc(debug=False, serial=False):
    nc = bass.Bass("TRN2", target_bir_lowering=False)

    d_x = nc.dram_tensor("x", [S_TOK, D], F32, kind="ExternalInput")
    d_wq = nc.dram_tensor("wq_t", [128, DK * 4096], F32, kind="ExternalInput")
    d_wk = nc.dram_tensor("wk_t", [128, DK * 128], F32, kind="ExternalInput")
    d_wv = nc.dram_tensor("wv_t", [128, DK * 128], F32, kind="ExternalInput")
    d_bq = nc.dram_tensor("bq", [128, 32], F32, kind="ExternalInput")
    d_bk = nc.dram_tensor("bk", [128, 1], F32, kind="ExternalInput")
    d_bv = nc.dram_tensor("bv", [128, 1], F32, kind="ExternalInput")
    d_gp = nc.dram_tensor("gate_p", [128, TI * PACK_GU], I32, kind="ExternalInput")
    d_up = nc.dram_tensor("up_p", [128, TI * PACK_GU], I32, kind="ExternalInput")
    d_dp = nc.dram_tensor("down_p", [128, HT * PACK_D], I32, kind="ExternalInput")
    d_scg = nc.dram_tensor("sc_g", [128, TI * 64], F32, kind="ExternalInput")
    d_zrg = nc.dram_tensor("zr_g", [128, TI * 64], F32, kind="ExternalInput")
    d_scu = nc.dram_tensor("sc_u", [128, TI * 64], F32, kind="ExternalInput")
    d_zru = nc.dram_tensor("zr_u", [128, TI * 64], F32, kind="ExternalInput")
    d_scd = nc.dram_tensor("sc_d", [128, HT * 172], F32, kind="ExternalInput")
    d_zrd = nc.dram_tensor("zr_d", [128, HT * 172], F32, kind="ExternalInput")
    d_id = nc.dram_tensor("ident", [128, 128], BF16, kind="ExternalInput")
    d_ones = nc.dram_tensor("onesr", [1, 128], BF16, kind="ExternalInput")
    d_bigo = nc.dram_tensor("bigo", [128, 256], BF16, kind="ExternalInput")

    d_y = nc.dram_tensor("y", [S_TOK, D], F32, kind="ExternalOutput")
    ik = "ExternalOutput" if debug else "Internal"
    d_x1 = nc.dram_tensor("x1d", [S_TOK, D], F32, kind=ik)
    d_act = nc.dram_tensor("actd", [INTER, S_TOK], BF16, kind=ik)
    d_wdt = nc.dram_tensor("wdtd", [128, HT * TI * 128], BF16, kind=ik)
    d_dbg = nc.dram_tensor("dbgd", [NT * 128, 448], F32, kind=ik) if debug         else None
    d_agd = nc.dram_tensor("agd", [172, S_TOK], BF16, kind=ik) if debug else None

    sb_id = nc.alloc_sbuf_tensor("sb_id", [128, 128], BF16)
    sb_ones = nc.alloc_sbuf_tensor("sb_ones", [1, 128], BF16)
    sb_bigo = nc.alloc_sbuf_tensor("sb_bigo", [128, 256], BF16)
    sb_x2t = nc.alloc_sbuf_tensor("sb_x2t", [128, DK * S_TOK], BF16)
    sb_xgT = nc.alloc_sbuf_tensor("sb_xgT", [128, S_TOK], BF16)
    sb_eps = nc.alloc_sbuf_tensor("sb_eps", [128, 1], F32)

    psum = nc.alloc_psum_tensor("psum", [128, 4096], F32)

    sems = {}
    cnt = {}
    for name in ("PE", "VE", "AC"):
        sems[name] = nc.alloc_semaphore("sem_" + name)
        cnt[name] = 0

    def inc(instr, name, k=1):
        instr.then_inc(sems[name], k)
        cnt[name] += k

    def incd(eng, name, k=1):
        eng.drain().then_inc(sems[name], k)
        cnt[name] += k

    def wait(eng, name, val):
        eng.wait_ge(sems[name], val)

    class DmaStream:

        def __init__(self, name, k=4):
            self.sems = [nc.alloc_semaphore(f"q_{name}{i}") for i in range(k)]
            self.cnt = [0] * k
            self.i = 0

        def issue(self, instr, gslot=None):
            r = (self.i if gslot is None else gslot) % len(self.sems)
            instr.then_inc(self.sems[r], 16)
            self.cnt[r] += 16
            if gslot is None:
                self.i += 1
            return (r, self.cnt[r])

        def wait(self, eng, tok):
            eng.wait_ge(self.sems[tok[0]], tok[1])

        def wait_all(self, eng):
            for s, c in zip(self.sems, self.cnt):
                if c:
                    eng.wait_ge(s, c)

    sax = DmaStream("ax", 4)
    swq = DmaStream("wq", 4)
    sx = DmaStream("x", 3)
    sx1s = DmaStream("x1s", 3)
    sgp = DmaStream("gp", 4)
    sas = DmaStream("as", 4)
    sdp = DmaStream("dp", 4)
    sws = DmaStream("ws", 4)
    sal = DmaStream("al", 8)
    swl = DmaStream("wl", 4)
    sxl = DmaStream("xl", 4)
    sy = DmaStream("y", 4)
    sdbg = DmaStream("dbg", 2)
    ALL_STREAMS = [sax, swq, sx, sx1s, sgp, sas, sdp, sws, sal, swl, sxl, sy,
                   sdbg]

    stack = contextlib.ExitStack()

    with nc.Block() as block:
        def _gate(fn):
            def wrapper(eng):
                for nm in ("PE", "VE", "AC"):
                    eng.wait_ge(sems[nm], cnt[nm])
                fn(eng)
            return wrapper

        _orig = {}
        for _nm in ("sync", "vector", "scalar", "tensor", "gpsimd"):
            _orig[_nm] = getattr(block, _nm)

        class _BlockProxy:
            pass

        if serial:
            bp = _BlockProxy()
            for _nm in ("vector", "scalar", "tensor"):
                setattr(bp, _nm, (lambda f, _o=_orig[_nm]: _o(_gate(f))))
            for _nm in ("sync", "gpsimd"):
                setattr(bp, _nm, _orig[_nm])
            block = bp
        pa = contextlib.ExitStack()
        sb_wqkv = pa.enter_context(nc.sbuf_tensor("sb_wqkv", [128, DK * 384], BF16))
        sb_brow = pa.enter_context(nc.sbuf_tensor("sb_brow", [1, 384], BF16))
        p0 = contextlib.ExitStack()
        sb_wqbuf = p0.enter_context(nc.sbuf_tensor("sb_wqbuf", [128, 2 * 4096], F32))
        sb_wkv = p0.enter_context(nc.sbuf_tensor("sb_wkv", [128, 2 * DK * 128], F32))
        sb_bias = p0.enter_context(nc.sbuf_tensor("sb_bias", [128, 36], F32))
        sb_b3 = p0.enter_context(nc.sbuf_tensor("sb_b3", [128, 3], BF16))
        sb_wqtmp = p0.enter_context(nc.sbuf_tensor("sb_wqtmp", [128, 128], F32))

        def p0_eps(vector):
            vector.memset(sb_eps[:], EPS)

        block.vector(p0_eps)

        wq_toks = {kt: (kt % 4, 16 * (kt // 4 + 1)) for kt in range(DK)}

        def p0_loads(sync):
            for dst, src in [(sb_id[:], d_id[:]), (sb_ones[:], d_ones[:]),
                             (sb_bigo[:], d_bigo[:]),
                             (sb_bias[:, 0:32], d_bq[:]),
                             (sb_bias[:, 32:33], d_bk[:]),
                             (sb_bias[:, 33:34], d_bv[:]),
                             (sb_wkv[:, 0:DK * 128], d_wk[:]),
                             (sb_wkv[:, DK * 128:2 * DK * 128], d_wv[:])]:
                aux_tok[0] = sax.issue(sync.dma_start(dst, src), gslot=0)
            for kt in range(2):
                t_ = swq.issue(
                    sync.dma_start(sb_wqbuf[:, kt * 4096:(kt + 1) * 4096],
                                   d_wq[:, kt * 4096:(kt + 1) * 4096]))
                assert t_ == wq_toks[kt]

        aux_tok = [None]
        block.sync(p0_loads)

        def p0_wq(vector):
            sax.wait(vector, aux_tok[0])
            for kt in range(DK):
                swq.wait(vector, wq_toks[kt])
                buf = sb_wqbuf[:, (kt % 2) * 4096:(kt % 2) * 4096 + 4096]
                vector.reduce_sum(sb_wqtmp[:],
                                  buf.rearrange("p (d h) -> p d h", d=128),
                                  axis=AX.X)
                vector.drain()
                ii = vector.tensor_scalar(
                    out=sb_wqkv[:, kt * 384:kt * 384 + 128], in0=sb_wqtmp[:],
                    scalar1=QKV_SCALE, scalar2=None, op0=ALU.mult)
                incd(vector, "VE", 1)

        block.vector(p0_wq)

        def p0_wq_loads(sync):
            for kt in range(2, DK):
                wait(sync, "VE", kt - 1)
                t_ = swq.issue(sync.dma_start(
                    sb_wqbuf[:, (kt % 2) * 4096:(kt % 2) * 4096 + 4096],
                    d_wq[:, kt * 4096:(kt + 1) * 4096]))
                assert t_ == wq_toks[kt]

        block.sync(p0_wq_loads)

        def p0_misc(vector):
            for kt in range(DK):
                vector.tensor_copy(sb_wqkv[:, kt * 384 + 128:kt * 384 + 256],
                                   sb_wkv[:, kt * 128:(kt + 1) * 128])
                vector.tensor_copy(sb_wqkv[:, kt * 384 + 256:kt * 384 + 384],
                                   sb_wkv[:, DK * 128 + kt * 128:
                                          DK * 128 + (kt + 1) * 128])
            vector.reduce_sum(sb_bias[:, 34:35],
                              sb_bias[:, 0:32].rearrange("p (a h) -> p a h", a=1),
                              axis=AX.X)
            vector.drain()
            vector.tensor_scalar(out=sb_b3[:, 0:1], in0=sb_bias[:, 34:35],
                                 scalar1=QKV_SCALE, scalar2=None, op0=ALU.mult)
            vector.tensor_copy(sb_b3[:, 1:2], sb_bias[:, 32:33])
            ii = vector.tensor_copy(sb_b3[:, 2:3], sb_bias[:, 33:34])
            incd(vector, "VE", 1)

        block.vector(p0_misc)
        ve_p0 = cnt["VE"]

        def p0_pe_bias(tensor):
            wait(tensor, "VE", ve_p0)
            ii = None
            for c in range(3):
                ii = tensor.matmul(psum[0:1, c * 128:(c + 1) * 128],
                                   lhsT=sb_b3[:, c:c + 1], rhs=sb_id[:],
                                   start=True, stop=True)
            incd(tensor, "PE", 1)

        block.tensor(p0_pe_bias)
        pe_bias = cnt["PE"]

        def p0_ve_bias(vector):
            wait(vector, "PE", pe_bias)
            ii = vector.tensor_copy(sb_brow[0:1, :], psum[0:1, 0:384])
            incd(vector, "VE", 1)

        block.vector(p0_ve_bias)
        ve_p0b = cnt["VE"]
        pe_p0 = cnt["PE"]
        p0.close()

        sb_xbuf = pa.enter_context(nc.sbuf_tensor("sb_xbuf", [128, 2 * D], F32))
        sb_x1 = pa.enter_context(nc.sbuf_tensor("sb_x1", [128, D], F32))
        sb_x1t = pa.enter_context(nc.sbuf_tensor("sb_x1t", [128, DK * 128], BF16))
        sb_xln = pa.enter_context(nc.sbuf_tensor("sb_xln", [128, D], BF16))
        sb_scA = pa.enter_context(nc.sbuf_tensor("sb_scA", [128, 4096], F32))
        sb_scB = pa.enter_context(nc.sbuf_tensor("sb_scB", [128, 4096], F32))
        sb_qkv = pa.enter_context(nc.sbuf_tensor("sb_qkv", [128, 384], F32))
        sb_sm = pa.enter_context(nc.sbuf_tensor("sb_sm", [128, 16], F32))
        sb_M = pa.enter_context(nc.sbuf_tensor("sb_M", [128, 128], F32))
        sb_Mt1 = pa.enter_context(nc.sbuf_tensor("sb_Mt1", [128, 128], F32))
        sb_Mt2 = pa.enter_context(nc.sbuf_tensor("sb_Mt2", [128, 128], F32))
        sb_S = pa.enter_context(nc.sbuf_tensor("sb_S", [128, 128], F32))
        sb_o = pa.enter_context(nc.sbuf_tensor("sb_o", [128, 128], F32))
        sb_xg = pa.enter_context(nc.sbuf_tensor("sb_xg", [128, 64], F32))
        sb_red = pa.enter_context(nc.sbuf_tensor("sb_red", [128, 32], F32))
        sb_xgb = pa.enter_context(nc.sbuf_tensor("sb_xgb", [128, 64], BF16))

        dbg_slot = [None]
        a_prev_set = [None]
        a_x_done = []
        a_qkv_evac = []
        a_x1_store = []

        def ln_emit(src_ap, dump_ap):
            def vfn(vector):
                vector.tensor_tensor(out=dump_ap, in0=src_ap, in1=src_ap,
                                     op=ALU.mult)
                vector.reduce_sum(sb_sm[:, 0:1],
                                  src_ap.rearrange("p (a q) -> p a q", a=1),
                                  axis=AX.X)
                vector.reduce_sum(sb_red[:, 0:32],
                                  dump_ap.rearrange("p (g q) -> p g q", g=32),
                                  axis=AX.X)
                vector.drain()
                vector.reduce_sum(sb_sm[:, 1:2],
                                  sb_red[:, 0:32].rearrange(
                                      "p (a q) -> p a q", a=1), axis=AX.X)
                vector.tensor_scalar(out=sb_sm[:, 2:3], in0=sb_sm[:, 0:1],
                                     scalar1=1.0 / D, scalar2=None, op0=ALU.mult)
                vector.tensor_scalar(out=sb_sm[:, 8:9], in0=sb_sm[:, 1:2],
                                     scalar1=1.0 / D, scalar2=None, op0=ALU.mult)
                vector.drain()
                vector.tensor_tensor(out=sb_sm[:, 9:10], in0=sb_sm[:, 2:3],
                                     in1=sb_sm[:, 2:3], op=ALU.mult)
                vector.drain()
                vector.tensor_tensor(out=sb_sm[:, 3:4], in0=sb_sm[:, 8:9],
                                     in1=sb_sm[:, 9:10], op=ALU.subtract)
                vector.drain()
                ii = vector.tensor_scalar(out=sb_sm[:, 10:11],
                                          in0=sb_sm[:, 3:4], scalar1=0.0,
                                          scalar2=None, op0=ALU.max)
                incd(vector, "VE", 1)

            block.vector(vfn)
            vm = cnt["VE"]

            if debug and dbg_slot[0] is not None:
                st_, = dbg_slot

                def _dump_ln(sync, st_=st_, vm=vm):
                    wait(sync, "VE", vm)
                    sdbg.issue(sync.dma_start(
                        d_dbg[st_ * 128:(st_ + 1) * 128, 400:416], sb_sm[:]))
                    sdbg.issue(sync.dma_start(
                        d_dbg[st_ * 128:(st_ + 1) * 128, 416:448],
                        sb_xbuf[:, (st_ % 2) * D:(st_ % 2) * D + 32]))

                block.sync(_dump_ln)
                dbg_slot[0] = None

            def sfn(scalar):
                wait(scalar, "VE", vm)
                scalar.activation(sb_sm[:, 4:5], sb_sm[:, 10:11], AF.Ln, bias=sb_eps[:])
                ii = scalar.activation(sb_sm[:, 5:6], sb_sm[:, 4:5], AF.Exp,
                                       scale=-0.5)
                incd(scalar, "AC", 1)

            block.scalar(sfn)
            return cnt["AC"]

        for st in range(NT):
            x_tok = [None]
            xb = sb_xbuf[:, (st % 2) * D:(st % 2) * D + D]

            def a_load(sync, st=st, x_tok=x_tok, xb=xb):
                if st == 0:
                    wait(sync, "VE", ve_p0b)
                    wait(sync, "PE", pe_p0)
                if st >= 2:
                    wait(sync, "VE", a_x_done[st - 2])
                x_tok[0] = sx.issue(
                    sync.dma_start(xb, d_x[st * 128:(st + 1) * 128, :]))

            block.sync(a_load)

            def a_gate(vector, st=st, x_tok=x_tok):
                sx.wait(vector, x_tok[0])
                if st == 0:
                    wait(vector, "PE", pe_p0)
                else:
                    sx1s.wait(vector, a_x1_store[st - 1])

            block.vector(a_gate)

            dbg_slot[0] = st
            ac_rstd1 = ln_emit(xb, sb_scB[:])

            def a_apply1(vector, ac=ac_rstd1, xb=xb):
                wait(vector, "AC", ac)
                ii = vector.tensor_scalar(out=sb_xln[:], in0=xb,
                                          scalar1=sb_sm[:, 2:3],
                                          scalar2=sb_sm[:, 5:6],
                                          op0=ALU.subtract, op1=ALU.mult)
                incd(vector, "VE", 1)

            block.vector(a_apply1)
            ve_xln1 = cnt["VE"]

            a_tr_evac = []
            for w in range(4):
                pcol = 512 + (w % 2) * 1024

                def a_tr(tensor, w=w, pcol=pcol, ve_xln1=ve_xln1, st=st):
                    if w == 0:
                        wait(tensor, "VE", ve_xln1)
                    if w < 2 and a_prev_set[0] is not None:
                        nm_, mk_ = a_prev_set[0][w + 2]
                        wait(tensor, nm_, mk_)
                    if w >= 2:
                        nm_, mk_ = a_tr_evac[w - 2]
                        wait(tensor, nm_, mk_)
                    ii = None
                    for j in range(8):
                        kt = w * 8 + j
                        ii = tensor.matmul(
                            psum[:, pcol + j * 128:pcol + (j + 1) * 128],
                            lhsT=sb_xln[:, kt * 128:(kt + 1) * 128],
                            rhs=sb_id[:], start=True, stop=True)
                    inc(ii, "PE", 1)

                block.tensor(a_tr)
                pm = cnt["PE"]

                if w % 2 == 0:
                    def a_tr_ev(vector, w=w, pcol=pcol, pm=pm):
                        wait(vector, "PE", pm)
                        vector.tensor_copy(sb_x1t[:, w * 1024:(w + 1) * 1024],
                                           psum[:, pcol:pcol + 1024])
                        incd(vector, "VE", 1)

                    block.vector(a_tr_ev)
                    a_tr_evac.append(("VE", cnt["VE"]))
                else:
                    def a_tr_ev(scalar, w=w, pcol=pcol, pm=pm):
                        wait(scalar, "PE", pm)
                        scalar.copy(sb_x1t[:, w * 1024:(w + 1) * 1024],
                                    psum[:, pcol:pcol + 1024])
                        incd(scalar, "AC", 1)

                    block.scalar(a_tr_ev)
                    a_tr_evac.append(("AC", cnt["AC"]))

            a_prev_set[0] = a_tr_evac
            ve_x1t = max(mk for nm, mk in a_tr_evac if nm == "VE")
            ac_x1t = max(mk for nm, mk in a_tr_evac if nm == "AC")

            def a_qkv(tensor, ve_x1t=ve_x1t, ac_x1t=ac_x1t, st=st):
                wait(tensor, "VE", ve_x1t)
                wait(tensor, "AC", ac_x1t)
                if st > 0:
                    wait(tensor, "AC", a_qkv_evac[st - 1])
                for kt in range(DK):
                    tensor.matmul(psum[:, 0:384],
                                  lhsT=sb_x1t[:, kt * 128:(kt + 1) * 128],
                                  rhs=sb_wqkv[:, kt * 384:(kt + 1) * 384],
                                  start=(kt == 0), stop=False)
                ii = tensor.matmul(psum[:, 0:384], lhsT=sb_ones[0:1, :],
                                   rhs=sb_brow[0:1, :], start=False, stop=True)
                incd(tensor, "PE", 1)

            block.tensor(a_qkv)
            pe_qkv = cnt["PE"]

            def a_qkv_ev(scalar, pe_qkv=pe_qkv):
                wait(scalar, "PE", pe_qkv)
                ii = scalar.copy(sb_qkv[:], psum[:, 0:384])
                incd(scalar, "AC", 1)

            block.scalar(a_qkv_ev)
            a_qkv_evac.append(cnt["AC"])
            ac_qkv = cnt["AC"]

            def a_smprep(vector, ac_qkv=ac_qkv):
                wait(vector, "AC", ac_qkv)
                kk = sb_qkv[:, 128:256].rearrange("p (a q) -> p a q", a=1)
                vector.reduce_max(sb_sm[:, 6:7], kk, axis=AX.X)
                vector.tensor_reduce(sb_sm[:, 7:8], kk, axis=AX.X, op=ALU.min)
                vector.drain()
                vector.tensor_scalar(out=sb_Mt1[:], in0=sb_qkv[:, 0:128],
                                     scalar1=sb_sm[:, 6:7], scalar2=None,
                                     op0=ALU.mult)
                vector.tensor_scalar(out=sb_Mt2[:], in0=sb_qkv[:, 0:128],
                                     scalar1=sb_sm[:, 7:8], scalar2=None,
                                     op0=ALU.mult)
                vector.drain()
                ii = vector.tensor_tensor(out=sb_M[:], in0=sb_Mt1[:],
                                          in1=sb_Mt2[:], op=ALU.max)
                incd(vector, "VE", 1)

            block.vector(a_smprep)

            if debug:
                vm_dbg = cnt["VE"]

                def a_dbg(sync, st=st, vm_dbg=vm_dbg):
                    wait(sync, "VE", vm_dbg)
                    sdbg.issue(sync.dma_start(
                        d_dbg[st * 128:(st + 1) * 128, 0:384], sb_qkv[:]))
                    sdbg.issue(sync.dma_start(
                        d_dbg[st * 128:(st + 1) * 128, 384:400], sb_sm[:]))

                block.sync(a_dbg)

            for q in range(4):
                dsl = slice(q * 32, q * 32 + 32)

                def a_sm1(vector, dsl=dsl):
                    qs_b = sb_qkv[:, 0:128][:, dsl][:, :, None].broadcast_to(
                        [128, 32, 128])
                    k_b = sb_qkv[:, 128:256][:, None, :].broadcast_to(
                        [128, 32, 128])
                    m_b = sb_M[:, dsl][:, :, None].broadcast_to([128, 32, 128])
                    scA = sb_scA[:].rearrange("p (dd e) -> p dd e", dd=32)
                    scB = sb_scB[:].rearrange("p (dd e) -> p dd e", dd=32)
                    vector.tensor_tensor(out=scA, in0=qs_b, in1=k_b, op=ALU.mult)
                    vector.drain()
                    ii = vector.tensor_tensor(out=scB, in0=scA, in1=m_b,
                                              op=ALU.subtract)
                    incd(vector, "VE", 1)

                block.vector(a_sm1)
                vm = cnt["VE"]

                def a_sm2(scalar, vm=vm):
                    wait(scalar, "VE", vm)
                    ii = scalar.activation(sb_scA[:], sb_scB[:], AF.Exp)
                    incd(scalar, "AC", 1)

                block.scalar(a_sm2)
                am = cnt["AC"]

                def a_sm3(vector, dsl=dsl, am=am):
                    wait(vector, "AC", am)
                    scA = sb_scA[:].rearrange("p (dd e) -> p dd e", dd=32)
                    scB = sb_scB[:].rearrange("p (dd e) -> p dd e", dd=32)
                    v_b = sb_qkv[:, 256:384][:, None, :].broadcast_to(
                        [128, 32, 128])
                    vector.reduce_sum(sb_S[:, dsl], scA, axis=AX.X)
                    vector.tensor_tensor(out=scB, in0=scA, in1=v_b, op=ALU.mult)
                    vector.drain()
                    ii = vector.reduce_sum(sb_o[:, dsl], scB, axis=AX.X)
                    incd(vector, "VE", 1)

                block.vector(a_sm3)

            def a_resid(vector, xb=xb):
                vector.reciprocal(sb_Mt1[:], sb_S[:])
                vector.drain()
                vector.tensor_tensor(out=sb_Mt2[:], in0=sb_o[:], in1=sb_Mt1[:],
                                     op=ALU.mult)
                vector.drain()
                vector.tensor_tensor(
                    out=sb_x1[:].rearrange("p (dd r) -> p dd r", r=32),
                    in0=xb.rearrange("p (dd r) -> p dd r", r=32),
                    in1=sb_Mt2[:, :, None].broadcast_to([128, 128, 32]),
                    op=ALU.add)
                incd(vector, "VE", 1)

            block.vector(a_resid)
            a_x_done.append(cnt["VE"])
            ve_x1 = cnt["VE"]

            def a_x1st(sync, st=st, ve_x1=ve_x1):
                wait(sync, "VE", ve_x1)
                a_x1_store.append(sx1s.issue(
                    sync.dma_start(d_x1[st * 128:(st + 1) * 128, :], sb_x1[:])))

            block.sync(a_x1st)

            ac_rstd2 = ln_emit(sb_x1[:], sb_scA[:])

            def a_apply2(vector, ac=ac_rstd2):
                wait(vector, "AC", ac)
                x1v = sb_x1[:].rearrange("p (c two) -> p c two", two=2)
                vector.tensor_scalar(out=sb_xln[:, 0:2048],
                                     in0=x1v[:, :, 0:1],
                                     scalar1=sb_sm[:, 2:3], scalar2=sb_sm[:, 5:6],
                                     op0=ALU.subtract, op1=ALU.mult)
                vector.tensor_scalar(out=sb_xln[:, 2048:4096],
                                     in0=x1v[:, :, 1:2],
                                     scalar1=sb_sm[:, 2:3], scalar2=sb_sm[:, 5:6],
                                     op0=ALU.subtract, op1=ALU.mult)
                vector.reduce_sum(sb_xg[:],
                                  sb_xln[:, 0:2048].rearrange(
                                      "p (g c) -> p g c", g=64), axis=AX.X)
                vector.reduce_sum(sb_scB[:, 0:64],
                                  sb_xln[:, 2048:4096].rearrange(
                                      "p (g c) -> p g c", g=64), axis=AX.X)
                vector.drain()
                ii = vector.tensor_tensor(out=sb_xgb[:], in0=sb_xg[:],
                                          in1=sb_scB[:, 0:64], op=ALU.add)
                incd(vector, "VE", 1)

            block.vector(a_apply2)
            ve_xln2 = cnt["VE"]

            a_tr2_evac = []
            for w in range(4):
                pcol = 512 + (w % 2) * 1024

                def a_tr2(tensor, w=w, pcol=pcol, ve_xln2=ve_xln2):
                    if w == 0:
                        wait(tensor, "VE", ve_xln2)
                    if w < 2 and a_prev_set[0] is not None:
                        nm_, mk_ = a_prev_set[0][w + 2]
                        wait(tensor, nm_, mk_)
                    if w >= 2:
                        nm_, mk_ = a_tr2_evac[w - 2]
                        wait(tensor, nm_, mk_)
                    ii = None
                    for j in range(8):
                        kt = w * 8 + j
                        ii = tensor.matmul(
                            psum[:, pcol + j * 128:pcol + (j + 1) * 128],
                            lhsT=sb_xln[:, kt * 128:(kt + 1) * 128],
                            rhs=sb_id[:], start=True, stop=True)
                    inc(ii, "PE", 1)

                block.tensor(a_tr2)
                pm = cnt["PE"]

                if w % 2 == 0:
                    def a_tr2_ev(vector, w=w, pcol=pcol, pm=pm, st=st):
                        wait(vector, "PE", pm)
                        dst = sb_x2t[:].rearrange(
                            "p (kt s) -> p kt s", s=S_TOK)[
                            :, w * 8:(w + 1) * 8, st * 128:(st + 1) * 128]
                        vector.tensor_copy(
                            dst, psum[:, pcol:pcol + 1024].rearrange(
                                "p (kt s) -> p kt s", s=128))
                        incd(vector, "VE", 1)

                    block.vector(a_tr2_ev)
                    a_tr2_evac.append(("VE", cnt["VE"]))
                else:
                    def a_tr2_ev(scalar, w=w, pcol=pcol, pm=pm, st=st):
                        wait(scalar, "PE", pm)
                        dst = sb_x2t[:].rearrange(
                            "p (kt s) -> p kt s", s=S_TOK)[
                            :, w * 8:(w + 1) * 8, st * 128:(st + 1) * 128]
                        scalar.copy(
                            dst, psum[:, pcol:pcol + 1024].rearrange(
                                "p (kt s) -> p kt s", s=128))
                        incd(scalar, "AC", 1)

                    block.scalar(a_tr2_ev)
                    a_tr2_evac.append(("AC", cnt["AC"]))

            a_prev_set[0] = a_tr2_evac
            ve_trdone = cnt["VE"]
            ac_trdone = max(mk for nm, mk in a_tr2_evac if nm == "AC")

            def a_xg_tr(tensor, ve_trdone=ve_trdone, ac_trdone=ac_trdone):
                wait(tensor, "VE", ve_trdone)
                wait(tensor, "AC", ac_trdone)
                ii = tensor.matmul(psum[0:64, 2560:2688], lhsT=sb_xgb[:],
                                   rhs=sb_id[:], start=True, stop=True)
                incd(tensor, "PE", 1)

            block.tensor(a_xg_tr)
            pm = cnt["PE"]

            def a_xg_ev(vector, st=st, pm=pm):
                wait(vector, "PE", pm)
                vector.tensor_copy(sb_xgT[0:64, st * 128:(st + 1) * 128],
                                   psum[0:64, 2560:2688])
                ii = vector.tensor_copy(sb_xgT[64:128, st * 128:(st + 1) * 128],
                                        psum[0:64, 2560:2688])
                incd(vector, "VE", 1)

            block.vector(a_xg_ev)

        ve_A = cnt["VE"]
        pe_A = cnt["PE"]
        pa.close()

        pb = contextlib.ExitStack()
        sb_scg = pb.enter_context(nc.sbuf_tensor("sb_scg", [128, TI * 64], BF16))
        sb_scu = pb.enter_context(nc.sbuf_tensor("sb_scu", [128, TI * 64], BF16))
        sb_zsT = pb.enter_context(nc.sbuf_tensor("sb_zsT", [128, TI * 128], BF16))

        pbp = contextlib.ExitStack()
        sb_sz = pbp.enter_context(nc.sbuf_tensor("sb_sz", [128, 2 * TI * 64], F32))
        sb_zs = pbp.enter_context(nc.sbuf_tensor("sb_zs", [128, TI * 64], BF16))
        sb_zs2 = pbp.enter_context(nc.sbuf_tensor("sb_zs2", [128, TI * 64], F32))

        ac_Aend = cnt["AC"]
        for wi, (dsc, dzr, scbf, roff) in enumerate(
                [(d_scg, d_zrg, sb_scg, 0), (d_scu, d_zru, sb_scu, 64)]):
            sz_tok = [None]

            def bp_load(sync, dsc=dsc, dzr=dzr, wi=wi, sz_tok=sz_tok):
                if wi == 0:
                    wait(sync, "VE", ve_A)
                    sx1s.wait_all(sync)
                else:
                    wait(sync, "VE", bp_ve_marks[wi - 1])
                sax.issue(sync.dma_start(sb_sz[:, 0:TI * 64], dsc[:]),
                          gslot=1 + wi)
                sz_tok[0] = sax.issue(
                    sync.dma_start(sb_sz[:, TI * 64:2 * TI * 64], dzr[:]),
                    gslot=1 + wi)

            block.sync(bp_load)

            def bp_ve(vector, scbf=scbf, sz_tok=sz_tok):
                sax.wait(vector, sz_tok[0])
                vector.tensor_copy(scbf[:], sb_sz[:, 0:TI * 64])
                vector.tensor_scalar(out=sb_zs2[:],
                                     in0=sb_sz[:, TI * 64:2 * TI * 64],
                                     scalar1=7.5, scalar2=-1.0,
                                     op0=ALU.subtract, op1=ALU.mult)
                vector.drain()
                ii = vector.tensor_tensor(out=sb_zs[:], in0=sb_zs2[:],
                                          in1=sb_sz[:, 0:TI * 64], op=ALU.mult)
                incd(vector, "VE", 1)

            block.vector(bp_ve)
            bp_ve_marks = bp_ve_marks if wi else []
            bp_ve_marks.append(cnt["VE"])
            ve_zs = cnt["VE"]

            bp_evacs = []
            for w0 in range(0, TI, 8):
                nw = min(8, TI - w0)
                widx = w0 // 8
                pcol = (widx % 2) * 1024

                def bp_tr(tensor, w0=w0, nw=nw, widx=widx, pcol=pcol,
                          ve_zs=ve_zs, wi=wi):
                    if widx == 0:
                        wait(tensor, "VE", ve_zs)
                        if wi == 0:
                            wait(tensor, "AC", ac_Aend)
                    if widx >= 2:
                        wait(tensor, "VE", bp_evacs[widx - 2])
                    ii = None
                    for j in range(nw):
                        t = w0 + j
                        ii = tensor.matmul(
                            psum[0:64, pcol + j * 128:pcol + (j + 1) * 128],
                            lhsT=sb_zs[:, t * 64:(t + 1) * 64], rhs=sb_id[:],
                            start=True, stop=True)
                    incd(tensor, "PE", 1)

                block.tensor(bp_tr)
                pm = cnt["PE"]

                def bp_ev(vector, w0=w0, nw=nw, pcol=pcol, pm=pm, roff=roff):
                    wait(vector, "PE", pm)
                    ii = vector.tensor_copy(
                        sb_zsT[roff:roff + 64, w0 * 128:(w0 + nw) * 128],
                        psum[0:64, pcol:pcol + nw * 128])
                    incd(vector, "VE", 1)

                block.vector(bp_ev)
                bp_evacs.append(cnt["VE"])

        pbp.close()
        ve_bp = cnt["VE"]
        pe_bp = cnt["PE"]

        sb_pk = pb.enter_context(nc.sbuf_tensor("sb_pk", [128, 4 * PACK_GU], I16))
        sb_hi = pb.enter_context(nc.sbuf_tensor("sb_hi", [128, PACK_GU], I16))
        sb_lo = pb.enter_context(nc.sbuf_tensor("sb_lo", [128, PACK_GU], I16))
        sb_wn = pb.enter_context(nc.sbuf_tensor("sb_wn", [128, 4 * D], BF16))
        sb_wt = pb.enter_context(nc.sbuf_tensor("sb_wt", [128, 4 * D], BF16))
        sb_silu = pb.enter_context(nc.sbuf_tensor("sb_silu", [128, 1024], F32))
        sb_actb = pb.enter_context(nc.sbuf_tensor("sb_actb", [128, 2 * 1024], BF16))

        b_deq_done = []
        b_tr_done = {}
        b_act_done = []
        b_act_store = []
        b_silu_marks = []
        b_wave_marks = {}
        b_pk_toks = []

        def b_emit_load(t):
            pk_tok = [None]

            def b_load(gpsimd, t=t, pk_tok=pk_tok):
                if t == 0:
                    wait(gpsimd, "VE", ve_bp)
                    sx1s.wait_all(gpsimd)
                if t >= 2:
                    wait(gpsimd, "VE", b_deq_done[t - 2])
                o = (t % 2) * PACK_GU
                sgp.issue(gpsimd.dma_start(
                    sb_pk[:, o:o + PACK_GU],
                    d_gp[:, t * PACK_GU:(t + 1) * PACK_GU]), gslot=t)
                pk_tok[0] = sgp.issue(gpsimd.dma_start(
                    sb_pk[:, 2 * PACK_GU + o:2 * PACK_GU + o + PACK_GU],
                    d_up[:, t * PACK_GU:(t + 1) * PACK_GU]), gslot=t)

            block.gpsimd(b_load)
            b_pk_toks.append(pk_tok)

        def b_emit_deq(t):
            def b_deq(vector, t=t):
                sgp.wait(vector, b_pk_toks[t][0])
                if t >= 2:
                    wait(vector, "PE", b_tr_done[t - 2])
                wb = (t % 2) * 2 * D
                for which in range(2):
                    dsc = sb_scg if which == 0 else sb_scu
                    o = (t % 2) * PACK_GU + which * 2 * PACK_GU
                    pk = sb_pk[:, o:o + PACK_GU]
                    vector.tensor_scalar(out=sb_hi[:], in0=pk, scalar1=4,
                                         scalar2=None,
                                         op0=ALU.logical_shift_right)
                    vector.tensor_scalar(out=sb_lo[:], in0=pk, scalar1=15,
                                         scalar2=None, op0=ALU.bitwise_and)
                    s_bc = dsc[:, t * 64:(t + 1) * 64][:, :, None].broadcast_to(
                        [128, 64, 32])
                    wv = sb_wn[:, wb + which * D:wb + which * D + D]
                    vector.scalar_tensor_tensor(
                        out=wv[:, 0:2048].rearrange("p (g c) -> p g c", g=64),
                        in0=sb_hi[:].rearrange("p (g c) -> p g c", g=64),
                        scalar=-7.5, op0=ALU.add, op1=ALU.mult, in1=s_bc)
                    vector.scalar_tensor_tensor(
                        out=wv[:, 2048:4096].rearrange("p (g c) -> p g c", g=64),
                        in0=sb_lo[:].rearrange("p (g c) -> p g c", g=64),
                        scalar=-7.5, op0=ALU.add, op1=ALU.mult, in1=s_bc)
                incd(vector, "VE", 1)

            block.vector(b_deq)
            b_deq_done.append(cnt["VE"])

        def b_emit_actmul(t):
            def b_actmul(vector, t=t):
                wait(vector, "AC", b_silu_marks[t])
                if t >= 2:
                    sas.wait(vector, b_act_store[t - 2])
                ab = sb_actb[:, (t % 2) * 1024:(t % 2) * 1024 + 1024]
                vector.tensor_tensor(out=ab, in0=sb_silu[:],
                                     in1=psum[:, 1024:2048], op=ALU.mult)
                incd(vector, "VE", 1)

            block.vector(b_actmul)
            b_act_done.append(cnt["VE"])
            vm = cnt["VE"]

            def b_actst(sync, t=t, vm=vm):
                wait(sync, "VE", vm)
                ab = sb_actb[:, (t % 2) * 1024:(t % 2) * 1024 + 1024]
                b_act_store.append(sas.issue(
                    sync.dma_start(d_act[t * 128:(t + 1) * 128, :], ab)))

            block.sync(b_actst)

        def b_emit_wave(tw, w):
            pcol = 2048 + (w % 2) * 1024
            wb = (tw % 2) * 2 * D
            ve_deq = b_deq_done[tw]

            def b_tr(tensor, tw=tw, w=w, pcol=pcol, ve_deq=ve_deq, wb=wb):
                if w == 0:
                    wait(tensor, "VE", ve_deq)
                    if tw == 0:
                        wait(tensor, "AC", ac_A)
                    if tw >= 1:
                        wait(tensor, "VE", b_wave_marks[tw - 1][6])
                if w == 1 and tw >= 1:
                    wait(tensor, "VE", b_wave_marks[tw - 1][7])
                if w >= 2:
                    wait(tensor, "VE", b_wave_marks[tw][w - 2])
                ii = None
                for j in range(8):
                    g = w * 8 + j
                    which, kt = divmod(g, 32)
                    ii = tensor.matmul(
                        psum[:, pcol + j * 128:pcol + (j + 1) * 128],
                        lhsT=sb_wn[:, wb + which * D + kt * 128:
                                   wb + which * D + (kt + 1) * 128],
                        rhs=sb_id[:], start=True, stop=True)
                inc(ii, "PE", 1)

            block.tensor(b_tr)
            pm = cnt["PE"]
            wtb = (tw % 2) * 2 * D

            def b_tr_ev(vector, w=w, pcol=pcol, pm=pm, wtb=wtb):
                wait(vector, "PE", pm)
                vector.tensor_copy(sb_wt[:, wtb + w * 1024:wtb + (w + 1) * 1024],
                                   psum[:, pcol:pcol + 1024])
                incd(vector, "VE", 1)

            block.vector(b_tr_ev)
            b_wave_marks.setdefault(tw, {})[w] = cnt["VE"]
            if w == 7:
                b_tr_done[tw] = pm

        def b_emit_mm(t, with_waves_of):
            wtb = (t % 2) * 2 * D
            jobs = []
            for which in range(2):
                roff = 0 if which == 0 else 64
                for half in range(2):
                    pc = which * 1024 + half * 512
                    for kt in range(DK):
                        jobs.append((pc, which * 32 + kt, half, kt == 0, False,
                                     None))
                    jobs.append((pc, None, half, False, True, roff))
            segs = [jobs[i * 17:(i + 1) * 17] for i in range(8)]
            ve_wt = max(b_wave_marks[t].values())

            for w in range(8):
                if with_waves_of is not None:
                    b_emit_wave(with_waves_of, w)

                def b_mm_seg(tensor, t=t, w=w, seg=segs[w], ve_wt=ve_wt,
                             wtb=wtb):
                    if w == 0:
                        wait(tensor, "VE", ve_wt)
                        if t > 0:
                            wait(tensor, "VE", b_act_done[t - 1])
                    ii = None
                    for pc, sl, half, st_, sp_, roff in seg:
                        if sl is not None:
                            ii = tensor.matmul(
                                psum[:, pc:pc + 512],
                                lhsT=sb_wt[:, wtb + sl * 128:wtb + (sl + 1) * 128],
                                rhs=sb_x2t[:, (sl % 32) * S_TOK + half * 512:
                                           (sl % 32) * S_TOK + half * 512 + 512],
                                start=st_, stop=False)
                        else:
                            ii = tensor.matmul(
                                psum[:, pc:pc + 512],
                                lhsT=sb_zsT[roff:roff + 64,
                                            t * 128:(t + 1) * 128],
                                rhs=sb_xgT[roff:roff + 64,
                                           half * 512:half * 512 + 512],
                                start=False, stop=True)
                    if w == 7:
                        incd(tensor, "PE", 1)
                    else:
                        inc(ii, "PE", 1)

                block.tensor(b_mm_seg)

            pm = cnt["PE"]

            def b_silu(scalar, pm=pm):
                wait(scalar, "PE", pm)
                scalar.activation(sb_silu[:], psum[:, 0:1024], AF.Silu)
                incd(scalar, "AC", 1)

            block.scalar(b_silu)
            b_silu_marks.append(cnt["AC"])

        ac_A = cnt["AC"]
        b_emit_load(0)
        b_emit_deq(0)
        b_emit_load(1)
        b_emit_deq(1)
        for w in range(8):
            b_emit_wave(0, w)
        for t in range(TI):
            if t >= 1:
                b_emit_actmul(t - 1)
            if t + 2 < TI:
                b_emit_load(t + 2)
                b_emit_deq(t + 2)
            b_emit_mm(t, t + 1 if t + 1 < TI else None)
        b_emit_actmul(TI - 1)

        pe_B = cnt["PE"]
        ve_B = cnt["VE"]
        pb.close()

        pco = contextlib.ExitStack()
        sb_zsdT1 = pco.enter_context(
            nc.sbuf_tensor("sb_zsdT1", [128, HT * 128], BF16))
        sb_zsdT2 = pco.enter_context(
            nc.sbuf_tensor("sb_zsdT2", [44, HT * 128], BF16))
        sb_actt = pco.enter_context(
            nc.sbuf_tensor("sb_actt", [128, 8 * 1024], BF16))
        sb_ag1 = pco.enter_context(nc.sbuf_tensor("sb_ag1", [128, S_TOK], BF16))
        sb_ag2 = pco.enter_context(nc.sbuf_tensor("sb_ag2", [44, S_TOK], BF16))
        pb2 = contextlib.ExitStack()
        sb_scd = pb2.enter_context(nc.sbuf_tensor("sb_scd", [128, HT * 172], BF16))
        pb2p = contextlib.ExitStack()
        sb_szd = pb2p.enter_context(nc.sbuf_tensor("sb_szd", [128, 2 * HT * 172], F32))
        sb_zsd = pb2p.enter_context(nc.sbuf_tensor("sb_zsd", [128, HT * 172], BF16))
        sb_zsd2 = pb2p.enter_context(nc.sbuf_tensor("sb_zsd2", [128, HT * 172], F32))

        szd_tok = [None]

        def b2p_load(sync):
            wait(sync, "VE", ve_B)
            sas.wait_all(sync)
            sax.issue(sync.dma_start(sb_szd[:, 0:HT * 172], d_scd[:]), gslot=3)
            szd_tok[0] = sax.issue(
                sync.dma_start(sb_szd[:, HT * 172:2 * HT * 172], d_zrd[:]),
                gslot=3)

        block.sync(b2p_load)

        def b2p_ve(vector):
            sax.wait(vector, szd_tok[0])
            vector.tensor_copy(sb_scd[:], sb_szd[:, 0:HT * 172])
            vector.tensor_scalar(out=sb_zsd2[:],
                                 in0=sb_szd[:, HT * 172:2 * HT * 172],
                                 scalar1=7.5, scalar2=-1.0,
                                 op0=ALU.subtract, op1=ALU.mult)
            vector.drain()
            ii = vector.tensor_tensor(out=sb_zsd[:], in0=sb_zsd2[:],
                                      in1=sb_szd[:, 0:HT * 172], op=ALU.mult)
            incd(vector, "VE", 1)

        block.vector(b2p_ve)
        ve_zsd = cnt["VE"]

        b2p_marks = []
        for ht in range(HT):
            def b2p_tr(tensor, ht=ht, ve_zsd=ve_zsd):
                if ht == 0:
                    wait(tensor, "VE", ve_zsd)
                if ht >= 2:
                    wait(tensor, "VE", b2p_marks[ht - 2])
                pcol = (ht % 2) * 512
                tensor.matmul(psum[0:128, pcol:pcol + 128],
                              lhsT=sb_zsd[:, ht * 172:ht * 172 + 128],
                              rhs=sb_id[:], start=True, stop=True)
                ii = tensor.matmul(psum[0:44, pcol + 128:pcol + 256],
                                   lhsT=sb_zsd[:, ht * 172 + 128:ht * 172 + 172],
                                   rhs=sb_id[:], start=True, stop=True)
                incd(tensor, "PE", 1)

            block.tensor(b2p_tr)
            pm = cnt["PE"]

            def b2p_ev(vector, ht=ht, pm=pm):
                wait(vector, "PE", pm)
                pcol = (ht % 2) * 512
                vector.tensor_copy(sb_zsdT1[:, ht * 128:(ht + 1) * 128],
                                   psum[0:128, pcol:pcol + 128])
                ii = vector.tensor_copy(sb_zsdT2[0:44, ht * 128:(ht + 1) * 128],
                                        psum[0:44, pcol + 128:pcol + 256])
                incd(vector, "VE", 1)

            block.vector(b2p_ev)
            b2p_marks.append(cnt["VE"])

        pb2p.close()
        ve_b2p = cnt["VE"]

        sb_pkd = pb2.enter_context(nc.sbuf_tensor("sb_pkd", [128, 2 * PACK_D], I16))
        sb_hid = pb2.enter_context(nc.sbuf_tensor("sb_hid", [128, PACK_D], I16))
        sb_lod = pb2.enter_context(nc.sbuf_tensor("sb_lod", [128, PACK_D], I16))
        sb_wnd = pb2.enter_context(nc.sbuf_tensor("sb_wnd", [128, INTER], BF16))
        sb_wtd = pb2.enter_context(nc.sbuf_tensor("sb_wtd", [128, TI * 128], BF16))

        b2_deq_done = []
        b2_tr_done = []
        b2_store_done = []

        c0_pe_marks = []
        act_toks = {}

        def emit_c0(t):
            def c0_load(sync, t=t):
                sas.wait(sync, b_act_store[t])
                if t >= 8:
                    wait(sync, "PE", c0_pe_marks[t - 8])
                o = (t % 8) * 1024
                act_toks[t] = sal.issue(
                    sync.dma_start(sb_actt[:, o:o + 1024],
                                   d_act[t * 128:(t + 1) * 128, :]))

            block.sync(c0_load)

            def c0_mm(tensor, t=t):
                sal.wait(tensor, act_toks[t])
                if t == 0:
                    wait(tensor, "VE", ve_B)
                o = (t % 8) * 1024
                if t < 32:
                    base, T, first, last = 2048, t, (t == 0), False
                elif t < 43:
                    base, T, first, last = 3072, t - 32, (t == 32), False
                elif t < 75:
                    base, T, first, last = 2048, t - 43, False, (t == 74)
                else:
                    base, T, first, last = 3072, t - 75, False, (t == 85)
                lhs = sb_bigo[:, 124 - 4 * T:252 - 4 * T]
                ii = None
                for half in range(2):
                    ii = tensor.matmul(
                        psum[:, base + half * 512:base + half * 512 + 512],
                        lhsT=lhs,
                        rhs=sb_actt[:, o + half * 512:o + half * 512 + 512],
                        start=first, stop=last)
                inc(ii, "PE", 1)

            block.tensor(c0_mm)
            c0_pe_marks.append(cnt["PE"])

        for ht in range(HT):
            pkd_tok = [None]

            def b2_load(gpsimd, ht=ht, pkd_tok=pkd_tok):
                if ht == 0:
                    wait(gpsimd, "VE", ve_b2p)
                    sas.wait_all(gpsimd)
                if ht >= 2:
                    wait(gpsimd, "VE", b2_deq_done[ht - 2])
                o = (ht % 2) * PACK_D
                pkd_tok[0] = sdp.issue(gpsimd.dma_start(
                    sb_pkd[:, o:o + PACK_D],
                    d_dp[:, ht * PACK_D:(ht + 1) * PACK_D]))

            block.gpsimd(b2_load)

            def b2_deq(vector, ht=ht, pkd_tok=pkd_tok):
                sdp.wait(vector, pkd_tok[0])
                if ht > 0:
                    wait(vector, "PE", b2_tr_done[ht - 1])
                o = (ht % 2) * PACK_D
                pk = sb_pkd[:, o:o + PACK_D]
                vector.tensor_scalar(out=sb_hid[:], in0=pk, scalar1=4,
                                     scalar2=None, op0=ALU.logical_shift_right)
                vector.tensor_scalar(out=sb_lod[:], in0=pk, scalar1=15,
                                     scalar2=None, op0=ALU.bitwise_and)
                s_bc = sb_scd[:, ht * 172:(ht + 1) * 172][:, :, None] \
                    .broadcast_to([128, 172, 32])
                vector.scalar_tensor_tensor(
                    out=sb_wnd[:, 0:PACK_D].rearrange("p (g c) -> p g c", g=172),
                    in0=sb_hid[:].rearrange("p (g c) -> p g c", g=172),
                    scalar=-7.5, op0=ALU.add, op1=ALU.mult, in1=s_bc)
                ii = vector.scalar_tensor_tensor(
                    out=sb_wnd[:, PACK_D:INTER].rearrange(
                        "p (g c) -> p g c", g=172),
                    in0=sb_lod[:].rearrange("p (g c) -> p g c", g=172),
                    scalar=-7.5, op0=ALU.add, op1=ALU.mult, in1=s_bc)
                incd(vector, "VE", 1)

            block.vector(b2_deq)
            b2_deq_done.append(cnt["VE"])
            ve_deq = cnt["VE"]

            b2_evacs = []
            for w0 in range(0, TI, 8):
                nw = min(8, TI - w0)
                widx = w0 // 8
                pcol = (widx % 2) * 1024

                def b2_tr(tensor, ht=ht, w0=w0, nw=nw, widx=widx, pcol=pcol,
                          ve_deq=ve_deq):
                    if widx == 0:
                        wait(tensor, "VE", ve_deq)
                        if ht >= 1:
                            sws.wait(tensor, b2_store_done[ht - 1])
                    if widx >= 2:
                        nm_, mk_ = b2_evacs[widx - 2]
                        wait(tensor, nm_, mk_)
                    ii = None
                    for j in range(nw):
                        kt = w0 + j
                        ii = tensor.matmul(
                            psum[:, pcol + j * 128:pcol + (j + 1) * 128],
                            lhsT=sb_wnd[:, kt * 128:(kt + 1) * 128],
                            rhs=sb_id[:], start=True, stop=True)
                    inc(ii, "PE", 1)

                block.tensor(b2_tr)
                pm = cnt["PE"]

                if widx % 2 == 0:
                    def b2_tr_ev(vector, w0=w0, nw=nw, pcol=pcol, pm=pm):
                        wait(vector, "PE", pm)
                        vector.tensor_copy(
                            sb_wtd[:, w0 * 128:(w0 + nw) * 128],
                            psum[:, pcol:pcol + nw * 128])
                        incd(vector, "VE", 1)

                    block.vector(b2_tr_ev)
                    b2_evacs.append(("VE", cnt["VE"]))
                else:
                    def b2_tr_ev(scalar, w0=w0, nw=nw, pcol=pcol, pm=pm):
                        wait(scalar, "PE", pm)
                        scalar.copy(
                            sb_wtd[:, w0 * 128:(w0 + nw) * 128],
                            psum[:, pcol:pcol + nw * 128])
                        incd(scalar, "AC", 1)

                    block.scalar(b2_tr_ev)
                    b2_evacs.append(("AC", cnt["AC"]))

            b2_tr_done.append(cnt["PE"])
            vm = cnt["VE"]
            am2 = cnt["AC"]

            def b2_st(sync, ht=ht, vm=vm, am2=am2):
                wait(sync, "VE", vm)
                wait(sync, "AC", am2)
                dst = d_wdt[:].rearrange("p (kt ht h) -> p kt ht h",
                                         kt=TI, ht=HT)[:, :, ht, :]
                b2_store_done.append(sws.issue(sync.dma_start(
                    dst, sb_wtd[:].rearrange("p (kt h) -> p kt h", kt=TI))))

            block.sync(b2_st)

            for t in range((TI * ht) // HT, (TI * (ht + 1)) // HT):
                emit_c0(t)

        pe_B2 = cnt["PE"]
        ve_B2 = cnt["VE"]
        pb2.close()

        pc_ = contextlib.ExitStack()

        def c0_fence(tensor):
            incd(tensor, "PE", 1)

        block.tensor(c0_fence)
        pe_c0 = cnt["PE"]

        def c0_ev(vector):
            wait(vector, "PE", pe_c0)
            vector.tensor_copy(sb_ag1[:], psum[:, 2048:3072])
            ii = vector.tensor_copy(sb_ag2[0:44, :], psum[0:44, 3072:4096])
            incd(vector, "VE", 1)

        block.vector(c0_ev)
        ve_ag = cnt["VE"]

        if debug:
            def c0_dump(sync):
                wait(sync, "VE", ve_ag)
                sdbg.issue(sync.dma_start(d_agd[0:128, :], sb_ag1[:]))
                sdbg.issue(sync.dma_start(d_agd[128:172, :], sb_ag2[0:44, :]))

            block.sync(c0_dump)

        sb_wdtb = pc_.enter_context(
            nc.sbuf_tensor("sb_wdtb", [128, 2 * 16 * 512], BF16))
        sb_x1c = pc_.enter_context(
            nc.sbuf_tensor("sb_x1c", [128, 2 * 8 * 512], F32))
        sb_y = pc_.enter_context(nc.sbuf_tensor("sb_y", [128, 8 * 512], F32))

        KT_BATCH = [list(range(b, min(b + 16, TI))) for b in range(0, TI, 16)]
        wdt_v = d_wdt[:].rearrange("p (kt ht h) -> p kt ht h", kt=TI, ht=HT)

        c_evac_marks = []
        c_x1_tok = {}
        c_y_tok = {}
        c_wdt_used = []
        c_act_used = []
        for hc in range(NHC):
            def c_x1load(sync, hc=hc):
                if hc == 0:
                    wait(sync, "VE", ve_ag)
                    sx1s.wait_all(sync)
                if hc >= 2:
                    wait(sync, "VE", c_evac_marks[hc - 2])
                o = (hc % 2) * 8 * 512
                for st2 in range(NT):
                    c_x1_tok[hc] = sxl.issue(sync.dma_start(
                        sb_x1c[:, o + st2 * 512:o + (st2 + 1) * 512],
                        d_x1[st2 * 128:(st2 + 1) * 128,
                             hc * 512:(hc + 1) * 512]), gslot=hc)

            block.sync(c_x1load)

            for bi, batch in enumerate(KT_BATCH):
                wdt_tok = [None]

                def c_wdtload(sync, hc=hc, bi=bi, batch=batch):
                    if hc == 0 and bi == 0:
                        sws.wait_all(sync)
                        sdp.wait_all(sync)
                    if len(c_wdt_used) >= 2:
                        wait(sync, "PE", c_wdt_used[-2])
                    o = (bi % 2) * 16 * 512
                    nb = len(batch)
                    src = wdt_v[:, batch[0]:batch[0] + nb,
                                hc * 4:(hc + 1) * 4, :]
                    dst = sb_wdtb[:, o:o + 16 * 512].rearrange(
                        "p (kt ht h) -> p kt ht h", kt=16, ht=4)[:, 0:nb, :, :]
                    wdt_tok[0] = swl.issue(sync.dma_start(dst, src))

                block.sync(c_wdtload)

                for kk, kt in enumerate(batch):
                    gidx = hc * TI + kt

                    ca_tok = [None]

                    def c_actload(sync, hc=hc, kt=kt, gidx=gidx, ca_tok=ca_tok):
                        if gidx < 8:
                            wait(sync, "PE", pe_c0)
                        else:
                            wait(sync, "PE", c_act_used[gidx - 8])
                        o = (gidx % 8) * 1024
                        ca_tok[0] = sal.issue(
                            sync.dma_start(sb_actt[:, o:o + 1024],
                                           d_act[kt * 128:(kt + 1) * 128, :]))

                    block.sync(c_actload)

                    def c_mm(tensor, hc=hc, bi=bi, kk=kk, kt=kt,
                             wdt_tok=wdt_tok, ca_tok=ca_tok):
                        swl.wait(tensor, wdt_tok[0])
                        sal.wait(tensor, ca_tok[0])
                        if hc == 0 and bi == 0 and kk == 0:
                            wait(tensor, "VE", ve_ag)
                        if hc > 0 and bi == 0 and kk == 0:
                            wait(tensor, "VE", c_evac_marks[hc - 1])
                        o = ((hc * TI + kt) % 8) * 1024
                        ob = (bi % 2) * 16 * 512
                        rhs = sb_wdtb[:, ob + kk * 512:ob + (kk + 1) * 512]
                        ii = None
                        for st2 in range(NT):
                            ii = tensor.matmul(
                                psum[:, st2 * 512:(st2 + 1) * 512],
                                lhsT=sb_actt[:, o + st2 * 128:
                                             o + (st2 + 1) * 128],
                                rhs=rhs,
                                start=(bi == 0 and kk == 0), stop=False)
                        inc(ii, "PE", 1)

                    block.tensor(c_mm)
                    c_act_used.append(cnt["PE"])

                c_wdt_used.append(cnt["PE"])

            def c_corr(tensor, hc=hc):
                ii = None
                for st2 in range(NT):
                    tensor.matmul(
                        psum[:, st2 * 512:(st2 + 1) * 512],
                        lhsT=sb_ag1[:, st2 * 128:(st2 + 1) * 128],
                        rhs=sb_zsdT1[:, hc * 512:(hc + 1) * 512],
                        start=False, stop=False)
                    ii = tensor.matmul(
                        psum[:, st2 * 512:(st2 + 1) * 512],
                        lhsT=sb_ag2[0:44, st2 * 128:(st2 + 1) * 128],
                        rhs=sb_zsdT2[0:44, hc * 512:(hc + 1) * 512],
                        start=False, stop=True)
                incd(tensor, "PE", 1)

            block.tensor(c_corr)
            pe_hc = cnt["PE"]

            def c_evac(vector, hc=hc, pe_hc=pe_hc):
                wait(vector, "PE", pe_hc)
                sxl.wait(vector, c_x1_tok[hc])
                if hc >= 1:
                    sy.wait(vector, c_y_tok[hc - 1])
                o = (hc % 2) * 8 * 512
                ii = None
                for st2 in range(NT):
                    ii = vector.tensor_tensor(
                        out=sb_y[:, st2 * 512:(st2 + 1) * 512],
                        in0=psum[:, st2 * 512:(st2 + 1) * 512],
                        in1=sb_x1c[:, o + st2 * 512:o + (st2 + 1) * 512],
                        op=ALU.add)
                incd(vector, "VE", 1)

            block.vector(c_evac)
            c_evac_marks.append(cnt["VE"])
            vm = cnt["VE"]

            def c_store(sync, hc=hc, vm=vm):
                wait(sync, "VE", vm)
                for st2 in range(NT):
                    c_y_tok[hc] = sy.issue(sync.dma_start(
                        d_y[st2 * 128:(st2 + 1) * 128,
                            hc * 512:(hc + 1) * 512],
                        sb_y[:, st2 * 512:(st2 + 1) * 512]), gslot=hc)

            block.sync(c_store)

        def fin(sync):
            for s_ in ALL_STREAMS:
                s_.wait_all(sync)

        block.sync(fin)
        pc_.close()
        pco.close()

    return nc


def kernel(L0_attn_x, L0_attn_Wq, L0_attn_bq, L0_attn_Wk, L0_attn_bk,
           L0_attn_Wv, L0_attn_bv, L0_ffn_gate_Wq, L0_ffn_up_Wq,
           L0_ffn_down_Wq, L0_ffn_gate_scale, L0_ffn_gate_zero,
           L0_ffn_up_scale, L0_ffn_up_zero, L0_ffn_down_scale,
           L0_ffn_down_zero):
    bf = ml_dtypes.bfloat16
    x = np.ascontiguousarray(np.asarray(L0_attn_x, dtype=np.float32)
                             .reshape(8192, 4096))
    perm = np.concatenate([np.arange(0, INTER, 2), np.arange(1, INTER, 2)])

    def tile_rows(a, ntile):
        a = np.ascontiguousarray(a)
        r, c = a.shape
        assert r == ntile * 128
        return np.ascontiguousarray(
            a.reshape(ntile, 128, c).transpose(1, 0, 2).reshape(128, ntile * c))

    gate_p = tile_rows(np.asarray(L0_ffn_gate_Wq, dtype=np.int32)[perm], TI)
    up_p = tile_rows(np.asarray(L0_ffn_up_Wq, dtype=np.int32)[perm], TI)
    down_p = tile_rows(np.asarray(L0_ffn_down_Wq, dtype=np.int32), HT)
    sc_g = tile_rows(np.asarray(L0_ffn_gate_scale, dtype=np.float32)[perm], TI)
    zr_g = tile_rows(np.asarray(L0_ffn_gate_zero, dtype=np.float32)[perm], TI)
    sc_u = tile_rows(np.asarray(L0_ffn_up_scale, dtype=np.float32)[perm], TI)
    zr_u = tile_rows(np.asarray(L0_ffn_up_zero, dtype=np.float32)[perm], TI)
    sc_d = tile_rows(np.asarray(L0_ffn_down_scale, dtype=np.float32), HT)
    zr_d = tile_rows(np.asarray(L0_ffn_down_zero, dtype=np.float32), HT)

    wq = np.asarray(L0_attn_Wq, dtype=np.float32).reshape(4096, 4096)
    wq_t = tile_rows(wq, DK)
    wk_t = tile_rows(np.asarray(L0_attn_Wk, dtype=np.float32)[:, :, 0], DK)
    wv_t = tile_rows(np.asarray(L0_attn_Wv, dtype=np.float32)[:, :, 0], DK)
    bq = np.ascontiguousarray(np.asarray(L0_attn_bq, dtype=np.float32))
    bk = np.ascontiguousarray(np.asarray(L0_attn_bk, dtype=np.float32))
    bv = np.ascontiguousarray(np.asarray(L0_attn_bv, dtype=np.float32))

    ident = np.eye(128, dtype=bf)
    onesr = np.ones((1, 128), dtype=bf)
    bigo = np.zeros((128, 256), dtype=np.float32)
    for p in range(128):
        bigo[p, 124 + p // 32] = 1.0
    bigo = np.ascontiguousarray(bigo.astype(bf))

    shared = {
        "wq_t": wq_t, "wk_t": wk_t, "wv_t": wv_t, "bq": bq, "bk": bk, "bv": bv,
        "gate_p": gate_p, "up_p": up_p, "down_p": down_p,
        "sc_g": sc_g, "zr_g": zr_g, "sc_u": sc_u, "zr_u": zr_u,
        "sc_d": sc_d, "zr_d": zr_d,
        "ident": ident, "onesr": onesr, "bigo": bigo,
    }
    in_maps = []
    for c in range(N_CORES):
        m = dict(shared)
        m["x"] = np.ascontiguousarray(x[c * S_TOK:(c + 1) * S_TOK])
        in_maps.append(m)

    nc = build_nc()
    res = run_bass_kernel_spmd(nc, in_maps, core_ids=list(range(N_CORES)))
    y = np.concatenate([np.asarray(r["y"]) for r in res.results], axis=0)
    return np.ascontiguousarray(y.reshape(1, 8192, 4096))


# version 13
# speedup vs baseline: 1.3116x; 1.0220x over previous
import contextlib
import numpy as np
import ml_dtypes
import concourse.bass as bass
import concourse.mybir as mybir
from concourse.bass_utils import run_bass_kernel_spmd

F32 = mybir.dt.float32
BF16 = mybir.dt.bfloat16
I32 = mybir.dt.int32
I16 = mybir.dt.int16
AF = mybir.ActivationFunctionType
ALU = mybir.AluOpType
AX = mybir.AxisListType

N_CORES = 8
S_TOK = 1024
NT = 8
D = 4096
DK = 32
INTER = 11008
TI = 86
PACK_GU = 2048
PACK_D = 5504
HT = 32
NHC = 8
QKV_SCALE = float(1.0 / np.sqrt(128.0))
EPS = 1e-5


def build_nc(debug=False, serial=False):
    nc = bass.Bass("TRN2", target_bir_lowering=False)

    d_x = nc.dram_tensor("x", [S_TOK, D], F32, kind="ExternalInput")
    d_wq = nc.dram_tensor("wq_t", [128, DK * 4096], F32, kind="ExternalInput")
    d_wk = nc.dram_tensor("wk_t", [128, DK * 128], F32, kind="ExternalInput")
    d_wv = nc.dram_tensor("wv_t", [128, DK * 128], F32, kind="ExternalInput")
    d_bq = nc.dram_tensor("bq", [128, 32], F32, kind="ExternalInput")
    d_bk = nc.dram_tensor("bk", [128, 1], F32, kind="ExternalInput")
    d_bv = nc.dram_tensor("bv", [128, 1], F32, kind="ExternalInput")
    d_gp = nc.dram_tensor("gate_p", [128, TI * PACK_GU], I32, kind="ExternalInput")
    d_up = nc.dram_tensor("up_p", [128, TI * PACK_GU], I32, kind="ExternalInput")
    d_dp = nc.dram_tensor("down_p", [128, HT * PACK_D], I32, kind="ExternalInput")
    d_scg = nc.dram_tensor("sc_g", [128, TI * 64], F32, kind="ExternalInput")
    d_zrg = nc.dram_tensor("zr_g", [128, TI * 64], F32, kind="ExternalInput")
    d_scu = nc.dram_tensor("sc_u", [128, TI * 64], F32, kind="ExternalInput")
    d_zru = nc.dram_tensor("zr_u", [128, TI * 64], F32, kind="ExternalInput")
    d_scd = nc.dram_tensor("sc_d", [128, HT * 172], F32, kind="ExternalInput")
    d_zrd = nc.dram_tensor("zr_d", [128, HT * 172], F32, kind="ExternalInput")
    d_id = nc.dram_tensor("ident", [128, 128], BF16, kind="ExternalInput")
    d_ones = nc.dram_tensor("onesr", [1, 128], BF16, kind="ExternalInput")
    d_bigo = nc.dram_tensor("bigo", [128, 256], BF16, kind="ExternalInput")

    d_y = nc.dram_tensor("y", [S_TOK, D], F32, kind="ExternalOutput")
    ik = "ExternalOutput" if debug else "Internal"
    d_x1 = nc.dram_tensor("x1d", [S_TOK, D], F32, kind=ik)
    d_act = nc.dram_tensor("actd", [INTER, S_TOK], BF16, kind=ik)
    d_wdt = nc.dram_tensor("wdtd", [128, HT * TI * 128], BF16, kind=ik)
    d_dbg = nc.dram_tensor("dbgd", [NT * 128, 704], F32, kind=ik) if debug         else None
    d_agd = nc.dram_tensor("agd", [172, S_TOK], BF16, kind=ik) if debug else None

    sb_id = nc.alloc_sbuf_tensor("sb_id", [128, 128], BF16)
    sb_ones = nc.alloc_sbuf_tensor("sb_ones", [1, 128], BF16)
    sb_bigo = nc.alloc_sbuf_tensor("sb_bigo", [128, 256], BF16)
    sb_x2t = nc.alloc_sbuf_tensor("sb_x2t", [128, DK * S_TOK], BF16)
    sb_xgT = nc.alloc_sbuf_tensor("sb_xgT", [128, S_TOK], BF16)
    sb_eps = nc.alloc_sbuf_tensor("sb_eps", [128, 1], F32)

    psum = nc.alloc_psum_tensor("psum", [128, 4096], F32)

    sems = {}
    cnt = {}
    for name in ("PE", "VE", "AC", "PO"):
        sems[name] = nc.alloc_semaphore("sem_" + name)
        cnt[name] = 0

    def inc(instr, name, k=1):
        instr.then_inc(sems[name], k)
        cnt[name] += k

    def incd(eng, name, k=1):
        eng.drain().then_inc(sems[name], k)
        cnt[name] += k

    def wait(eng, name, val):
        eng.wait_ge(sems[name], val)

    class DmaStream:

        def __init__(self, name, k=4):
            self.sems = [nc.alloc_semaphore(f"q_{name}{i}") for i in range(k)]
            self.cnt = [0] * k
            self.i = 0

        def issue(self, instr, gslot=None):
            r = (self.i if gslot is None else gslot) % len(self.sems)
            instr.then_inc(self.sems[r], 16)
            self.cnt[r] += 16
            if gslot is None:
                self.i += 1
            return (r, self.cnt[r])

        def wait(self, eng, tok):
            eng.wait_ge(self.sems[tok[0]], tok[1])

        def wait_all(self, eng):
            for s, c in zip(self.sems, self.cnt):
                if c:
                    eng.wait_ge(s, c)

    sax = DmaStream("ax", 4)
    swq = DmaStream("wq", 4)
    sx = DmaStream("x", 3)
    sx1s = DmaStream("x1s", 3)
    sgp = DmaStream("gp", 4)
    sas = DmaStream("as", 4)
    sdp = DmaStream("dp", 4)
    sws = DmaStream("ws", 4)
    sal = DmaStream("al", 8)
    swl = DmaStream("wl", 4)
    sxl = DmaStream("xl", 4)
    sy = DmaStream("y", 4)
    sdbg = DmaStream("dbg", 2)
    ALL_STREAMS = [sax, swq, sx, sx1s, sgp, sas, sdp, sws, sal, swl, sxl, sy,
                   sdbg]

    stack = contextlib.ExitStack()

    with nc.Block() as block:
        def _gate(fn):
            def wrapper(eng):
                for nm in ("PE", "VE", "AC", "PO"):
                    eng.wait_ge(sems[nm], cnt[nm])
                fn(eng)
            return wrapper

        _orig = {}
        for _nm in ("sync", "vector", "scalar", "tensor", "gpsimd"):
            _orig[_nm] = getattr(block, _nm)

        class _BlockProxy:
            pass

        if serial:
            bp = _BlockProxy()
            for _nm in ("vector", "scalar", "tensor", "gpsimd"):
                setattr(bp, _nm, (lambda f, _o=_orig[_nm]: _o(_gate(f))))
            for _nm in ("sync",):
                setattr(bp, _nm, _orig[_nm])
            block = bp
        pa = contextlib.ExitStack()
        sb_wqkv = pa.enter_context(nc.sbuf_tensor("sb_wqkv", [128, DK * 384], BF16))
        sb_brow = pa.enter_context(nc.sbuf_tensor("sb_brow", [1, 384], BF16))
        p0 = contextlib.ExitStack()
        sb_wqbuf = p0.enter_context(nc.sbuf_tensor("sb_wqbuf", [128, 2 * 4096], F32))
        sb_wkv = p0.enter_context(nc.sbuf_tensor("sb_wkv", [128, 2 * DK * 128], F32))
        sb_bias = p0.enter_context(nc.sbuf_tensor("sb_bias", [128, 36], F32))
        sb_b3 = p0.enter_context(nc.sbuf_tensor("sb_b3", [128, 3], BF16))
        sb_wqtmp = p0.enter_context(nc.sbuf_tensor("sb_wqtmp", [128, 128], F32))

        def p0_eps(vector):
            vector.memset(sb_eps[:], EPS)

        block.vector(p0_eps)

        wq_toks = {kt: (kt % 4, 16 * (kt // 4 + 1)) for kt in range(DK)}

        def p0_loads(sync):
            for dst, src in [(sb_id[:], d_id[:]), (sb_ones[:], d_ones[:]),
                             (sb_bigo[:], d_bigo[:]),
                             (sb_bias[:, 0:32], d_bq[:]),
                             (sb_bias[:, 32:33], d_bk[:]),
                             (sb_bias[:, 33:34], d_bv[:]),
                             (sb_wkv[:, 0:DK * 128], d_wk[:]),
                             (sb_wkv[:, DK * 128:2 * DK * 128], d_wv[:])]:
                aux_tok[0] = sax.issue(sync.dma_start(dst, src), gslot=0)
            for kt in range(2):
                t_ = swq.issue(
                    sync.dma_start(sb_wqbuf[:, kt * 4096:(kt + 1) * 4096],
                                   d_wq[:, kt * 4096:(kt + 1) * 4096]))
                assert t_ == wq_toks[kt]

        aux_tok = [None]
        block.sync(p0_loads)

        def p0_wq(vector):
            sax.wait(vector, aux_tok[0])
            for kt in range(DK):
                swq.wait(vector, wq_toks[kt])
                buf = sb_wqbuf[:, (kt % 2) * 4096:(kt % 2) * 4096 + 4096]
                vector.reduce_sum(sb_wqtmp[:],
                                  buf.rearrange("p (d h) -> p d h", d=128),
                                  axis=AX.X)
                vector.drain()
                ii = vector.tensor_scalar(
                    out=sb_wqkv[:, kt * 384:kt * 384 + 128], in0=sb_wqtmp[:],
                    scalar1=QKV_SCALE, scalar2=None, op0=ALU.mult)
                incd(vector, "VE", 1)

        block.vector(p0_wq)

        def p0_wq_loads(sync):
            for kt in range(2, DK):
                wait(sync, "VE", kt - 1)
                t_ = swq.issue(sync.dma_start(
                    sb_wqbuf[:, (kt % 2) * 4096:(kt % 2) * 4096 + 4096],
                    d_wq[:, kt * 4096:(kt + 1) * 4096]))
                assert t_ == wq_toks[kt]

        block.sync(p0_wq_loads)

        def p0_misc(vector):
            for kt in range(DK):
                vector.tensor_copy(sb_wqkv[:, kt * 384 + 128:kt * 384 + 256],
                                   sb_wkv[:, kt * 128:(kt + 1) * 128])
                vector.tensor_copy(sb_wqkv[:, kt * 384 + 256:kt * 384 + 384],
                                   sb_wkv[:, DK * 128 + kt * 128:
                                          DK * 128 + (kt + 1) * 128])
            vector.reduce_sum(sb_bias[:, 34:35],
                              sb_bias[:, 0:32].rearrange("p (a h) -> p a h", a=1),
                              axis=AX.X)
            vector.drain()
            vector.tensor_scalar(out=sb_b3[:, 0:1], in0=sb_bias[:, 34:35],
                                 scalar1=QKV_SCALE, scalar2=None, op0=ALU.mult)
            vector.tensor_copy(sb_b3[:, 1:2], sb_bias[:, 32:33])
            ii = vector.tensor_copy(sb_b3[:, 2:3], sb_bias[:, 33:34])
            incd(vector, "VE", 1)

        block.vector(p0_misc)
        ve_p0 = cnt["VE"]

        def p0_pe_bias(tensor):
            wait(tensor, "VE", ve_p0)
            ii = None
            for c in range(3):
                ii = tensor.matmul(psum[0:1, c * 128:(c + 1) * 128],
                                   lhsT=sb_b3[:, c:c + 1], rhs=sb_id[:],
                                   start=True, stop=True)
            incd(tensor, "PE", 1)

        block.tensor(p0_pe_bias)
        pe_bias = cnt["PE"]

        def p0_ve_bias(vector):
            wait(vector, "PE", pe_bias)
            ii = vector.tensor_copy(sb_brow[0:1, :], psum[0:1, 0:384])
            incd(vector, "VE", 1)

        block.vector(p0_ve_bias)
        ve_p0b = cnt["VE"]
        pe_p0 = cnt["PE"]
        p0.close()

        sb_xbuf = pa.enter_context(nc.sbuf_tensor("sb_xbuf", [128, 2 * D], F32))
        sb_x1 = pa.enter_context(nc.sbuf_tensor("sb_x1", [128, D], F32))
        sb_x1t = pa.enter_context(nc.sbuf_tensor("sb_x1t", [128, DK * 128], BF16))
        sb_xln = pa.enter_context(nc.sbuf_tensor("sb_xln", [128, D], BF16))
        sb_scA = pa.enter_context(nc.sbuf_tensor("sb_scA", [128, 4096], F32))
        sb_scA2 = pa.enter_context(nc.sbuf_tensor("sb_scA2", [128, 4096], F32))
        sb_scB = pa.enter_context(nc.sbuf_tensor("sb_scB", [128, 4096], F32))
        sb_qkv = pa.enter_context(nc.sbuf_tensor("sb_qkv", [128, 384], F32))
        sb_sm = pa.enter_context(nc.sbuf_tensor("sb_sm", [128, 16], F32))
        sb_Mt1 = pa.enter_context(nc.sbuf_tensor("sb_Mt1", [128, 128], F32))
        sb_Mt2 = pa.enter_context(nc.sbuf_tensor("sb_Mt2", [128, 128], F32))
        sb_S = pa.enter_context(nc.sbuf_tensor("sb_S", [128, 128], F32))
        sb_o = pa.enter_context(nc.sbuf_tensor("sb_o", [128, 128], F32))
        sb_xg = pa.enter_context(nc.sbuf_tensor("sb_xg", [128, 64], F32))
        sb_xgb = pa.enter_context(nc.sbuf_tensor("sb_xgb", [128, 64], BF16))

        dbg_slot = [None]
        a_prev_set = [None]
        a_o_marks = []
        a_pv_marks = []
        a_e_marks = []
        a_po_marks = []
        a_ss_marks = []
        a_x_done = []
        a_qkv_evac = []
        a_x1_store = []

        def ln_emit(src_ap, dump_ap, ac_wait=None):
            def vsum(vector):
                vector.reduce_sum(sb_sm[:, 0:1],
                                  src_ap.rearrange("p (a q) -> p a q", a=1),
                                  axis=AX.X)

            block.vector(vsum)

            def sqfn(scalar):
                if ac_wait is not None:
                    ac_wait(scalar)
                if a_pv_marks:
                    wait(scalar, "VE", a_pv_marks[-1])
                scalar.activation(dump_ap, src_ap, AF.Square,
                                  accum_out=sb_sm[:, 1:2])
                incd(scalar, "AC", 1)

            block.scalar(sqfn)
            a_ss_marks.append(cnt["AC"])
            ssm = cnt["AC"]

            def vfn(vector, ssm=ssm):
                wait(vector, "AC", ssm)
                vector.drain()
                vector.tensor_scalar(out=sb_sm[:, 2:3], in0=sb_sm[:, 0:1],
                                     scalar1=1.0 / D, scalar2=None, op0=ALU.mult)
                vector.drain()
                vector.tensor_scalar(out=sb_sm[:, 9:10], in0=sb_sm[:, 2:3],
                                     scalar1=sb_sm[:, 2:3], scalar2=None,
                                     op0=ALU.mult)
                vector.drain()
                vector.tensor_scalar(out=sb_sm[:, 3:4], in0=sb_sm[:, 1:2],
                                     scalar1=1.0 / D, scalar2=sb_sm[:, 9:10],
                                     op0=ALU.mult, op1=ALU.subtract)
                vector.drain()
                ii = vector.tensor_scalar(out=sb_sm[:, 10:11],
                                          in0=sb_sm[:, 3:4], scalar1=0.0,
                                          scalar2=None, op0=ALU.max)
                incd(vector, "VE", 1)

            block.vector(vfn)
            vm = cnt["VE"]

            if debug and dbg_slot[0] is not None:
                st_, = dbg_slot

                def _dump_ln(sync, st_=st_, vm=vm):
                    wait(sync, "VE", vm)
                    sdbg.issue(sync.dma_start(
                        d_dbg[st_ * 128:(st_ + 1) * 128, 400:416], sb_sm[:]))
                    sdbg.issue(sync.dma_start(
                        d_dbg[st_ * 128:(st_ + 1) * 128, 416:448],
                        sb_xbuf[:, (st_ % 2) * D:(st_ % 2) * D + 32]))

                block.sync(_dump_ln)
                dbg_slot[0] = None

            def sfn(scalar):
                wait(scalar, "VE", vm)
                scalar.activation(sb_sm[:, 4:5], sb_sm[:, 10:11], AF.Ln, bias=sb_eps[:])
                ii = scalar.activation(sb_sm[:, 5:6], sb_sm[:, 4:5], AF.Exp,
                                       scale=-0.5)
                incd(scalar, "AC", 1)

            block.scalar(sfn)
            return cnt["AC"]

        for st in range(NT):
            x_tok = [None]
            xb = sb_xbuf[:, (st % 2) * D:(st % 2) * D + D]

            def a_load(sync, st=st, x_tok=x_tok, xb=xb):
                if st == 0:
                    wait(sync, "VE", ve_p0b)
                    wait(sync, "PE", pe_p0)
                if st >= 2:
                    wait(sync, "VE", a_x_done[st - 2])
                    wait(sync, "AC", a_ss_marks[2 * (st - 2)])
                x_tok[0] = sx.issue(
                    sync.dma_start(xb, d_x[st * 128:(st + 1) * 128, :]))

            block.sync(a_load)

            def a_gate(vector, st=st, x_tok=x_tok):
                sx.wait(vector, x_tok[0])
                if st == 0:
                    wait(vector, "PE", pe_p0)
                else:
                    sx1s.wait(vector, a_x1_store[st - 1])

            block.vector(a_gate)

            dbg_slot[0] = st

            def _ln1_wait(scalar, x_tok=x_tok, st=st):
                sx.wait(scalar, x_tok[0])

            ac_rstd1 = ln_emit(xb, sb_scB[:], ac_wait=_ln1_wait)

            def a_apply1(vector, ac=ac_rstd1, xb=xb):
                wait(vector, "AC", ac)
                ii = vector.tensor_scalar(out=sb_xln[:], in0=xb,
                                          scalar1=sb_sm[:, 2:3],
                                          scalar2=sb_sm[:, 5:6],
                                          op0=ALU.subtract, op1=ALU.mult)
                incd(vector, "VE", 1)

            block.vector(a_apply1)
            ve_xln1 = cnt["VE"]

            a_tr_evac = []
            for w in range(4):
                pcol = 512 + (w % 2) * 1024

                def a_tr(tensor, w=w, pcol=pcol, ve_xln1=ve_xln1, st=st):
                    if w == 0:
                        wait(tensor, "VE", ve_xln1)
                    if w < 2 and a_prev_set[0] is not None:
                        nm_, mk_ = a_prev_set[0][w + 2]
                        wait(tensor, nm_, mk_)
                    if w >= 2:
                        nm_, mk_ = a_tr_evac[w - 2]
                        wait(tensor, nm_, mk_)
                    ii = None
                    for j in range(8):
                        kt = w * 8 + j
                        ii = tensor.matmul(
                            psum[:, pcol + j * 128:pcol + (j + 1) * 128],
                            lhsT=sb_xln[:, kt * 128:(kt + 1) * 128],
                            rhs=sb_id[:], start=True, stop=True)
                    inc(ii, "PE", 1)

                block.tensor(a_tr)
                pm = cnt["PE"]

                if w % 2 == 0:
                    def a_tr_ev(vector, w=w, pcol=pcol, pm=pm):
                        wait(vector, "PE", pm)
                        vector.tensor_copy(sb_x1t[:, w * 1024:(w + 1) * 1024],
                                           psum[:, pcol:pcol + 1024])
                        incd(vector, "VE", 1)

                    block.vector(a_tr_ev)
                    a_tr_evac.append(("VE", cnt["VE"]))
                else:
                    def a_tr_ev(scalar, w=w, pcol=pcol, pm=pm):
                        wait(scalar, "PE", pm)
                        scalar.copy(sb_x1t[:, w * 1024:(w + 1) * 1024],
                                    psum[:, pcol:pcol + 1024])
                        incd(scalar, "AC", 1)

                    block.scalar(a_tr_ev)
                    a_tr_evac.append(("AC", cnt["AC"]))

            a_prev_set[0] = a_tr_evac
            ve_x1t = max(mk for nm, mk in a_tr_evac if nm == "VE")
            ac_x1t = max(mk for nm, mk in a_tr_evac if nm == "AC")

            def a_qkv(tensor, ve_x1t=ve_x1t, ac_x1t=ac_x1t, st=st):
                wait(tensor, "VE", ve_x1t)
                wait(tensor, "AC", ac_x1t)
                if st > 0:
                    wait(tensor, "AC", a_qkv_evac[st - 1])
                for kt in range(DK):
                    tensor.matmul(psum[:, 0:384],
                                  lhsT=sb_x1t[:, kt * 128:(kt + 1) * 128],
                                  rhs=sb_wqkv[:, kt * 384:(kt + 1) * 384],
                                  start=(kt == 0), stop=False)
                ii = tensor.matmul(psum[:, 0:384], lhsT=sb_ones[0:1, :],
                                   rhs=sb_brow[0:1, :], start=False, stop=True)
                incd(tensor, "PE", 1)

            block.tensor(a_qkv)
            pe_qkv = cnt["PE"]

            def a_qkv_ev(scalar, pe_qkv=pe_qkv):
                wait(scalar, "PE", pe_qkv)
                if a_po_marks:
                    wait(scalar, "PO", a_po_marks[-1])
                if a_pv_marks:
                    wait(scalar, "VE", a_pv_marks[-1])
                ii = scalar.copy(sb_qkv[:], psum[:, 0:384])
                incd(scalar, "AC", 1)

            block.scalar(a_qkv_ev)
            a_qkv_evac.append(cnt["AC"])
            ac_qkv = cnt["AC"]

            g0 = st * 4
            scab = [sb_scA, sb_scA2]

            def qs_view(dsl):
                return sb_qkv[:, 0:128][:, dsl][:, :, None].broadcast_to(
                    [128, 32, 128])

            k_view = sb_qkv[:, 128:256][:, None, :].broadcast_to([128, 32, 128])
            v_view = sb_qkv[:, 256:384][:, None, :].broadcast_to([128, 32, 128])

            def a_pool(gpsimd, qs_, g0=g0, ac_qkv=ac_qkv):
                if qs_[0] == 0:
                    wait(gpsimd, "AC", ac_qkv)
                for q in qs_:
                    g = g0 + q
                    if g >= 2:
                        wait(gpsimd, "VE", a_o_marks[g - 2])
                    dst = scab[g % 2][:].rearrange("p (dd e) -> p dd e", dd=32)
                    gpsimd.tensor_tensor(out=dst, in0=qs_view(
                        slice(q * 32, q * 32 + 32)), in1=k_view, op=ALU.mult)
                incd(gpsimd, "PO", len(qs_))

            def a_exp(scalar, q, g0=g0):
                g = g0 + q
                wait(scalar, "PO", a_po_marks[g])
                if g >= 1:
                    wait(scalar, "VE", a_pv_marks[g - 1])
                ii = scalar.activation(sb_scB[:], scab[g % 2][:], AF.Exp)
                incd(scalar, "AC", 1)

            def a_red(vector, q, g0=g0):
                g = g0 + q
                wait(vector, "AC", a_e_marks[g])
                dsl = slice(q * 32, q * 32 + 32)
                vector.reduce_sum(
                    sb_S[:, dsl],
                    sb_scB[:].rearrange("p (dd e) -> p dd e", dd=32), axis=AX.X)
                vector.tensor_tensor(
                    out=scab[g % 2][:].rearrange("p (dd e) -> p dd e", dd=32),
                    in0=sb_scB[:].rearrange("p (dd e) -> p dd e", dd=32),
                    in1=v_view, op=ALU.mult)

            def emit_o(vector, q, g0=g0):
                g = g0 + q
                dsl = slice(q * 32, q * 32 + 32)
                vector.reduce_sum(
                    sb_o[:, dsl],
                    scab[g % 2][:].rearrange("p (dd e) -> p dd e", dd=32),
                    axis=AX.X)

            block.gpsimd(lambda gpsimd: a_pool(gpsimd, (0, 1)))
            a_po_marks.extend([cnt["PO"]] * 2)
            block.scalar(lambda scalar: a_exp(scalar, 0))
            a_e_marks.append(cnt["AC"])
            block.vector(lambda vector: (a_red(vector, 0),
                                         incd(vector, "VE", 1)))
            a_pv_marks.append(cnt["VE"])
            block.scalar(lambda scalar: a_exp(scalar, 1))
            a_e_marks.append(cnt["AC"])
            block.vector(lambda vector: (a_red(vector, 1), emit_o(vector, 0),
                                         incd(vector, "VE", 1)))
            a_pv_marks.append(cnt["VE"])
            a_o_marks.append(cnt["VE"])
            block.gpsimd(lambda gpsimd: a_pool(gpsimd, (2,)))
            a_po_marks.append(cnt["PO"])
            block.scalar(lambda scalar: a_exp(scalar, 2))
            a_e_marks.append(cnt["AC"])
            block.vector(lambda vector: (a_red(vector, 2), emit_o(vector, 1),
                                         incd(vector, "VE", 1)))
            a_pv_marks.append(cnt["VE"])
            a_o_marks.append(cnt["VE"])
            block.gpsimd(lambda gpsimd: a_pool(gpsimd, (3,)))
            a_po_marks.append(cnt["PO"])
            block.scalar(lambda scalar: a_exp(scalar, 3))
            a_e_marks.append(cnt["AC"])

            def a_tail(vector, xb=xb):
                a_red(vector, 3)
                vector.reciprocal(sb_Mt1[:], sb_S[:])
                emit_o(vector, 2)
                emit_o(vector, 3)
                vector.drain()
                vector.tensor_tensor(out=sb_Mt2[:], in0=sb_o[:], in1=sb_Mt1[:],
                                     op=ALU.mult)
                vector.drain()
                vector.tensor_tensor(
                    out=sb_x1[:].rearrange("p (dd r) -> p dd r", r=32),
                    in0=xb.rearrange("p (dd r) -> p dd r", r=32),
                    in1=sb_Mt2[:, :, None].broadcast_to([128, 128, 32]),
                    op=ALU.add)
                incd(vector, "VE", 1)

            block.vector(a_tail)
            if debug:
                _vtail = cnt["VE"]

                def a_dbg2(sync, st=st, _vtail=_vtail):
                    wait(sync, "VE", _vtail)
                    sdbg.issue(sync.dma_start(
                        d_dbg[st * 128:(st + 1) * 128, 0:384], sb_qkv[:]))
                    sdbg.issue(sync.dma_start(
                        d_dbg[st * 128:(st + 1) * 128, 448:576], sb_S[:]))
                    sdbg.issue(sync.dma_start(
                        d_dbg[st * 128:(st + 1) * 128, 576:704], sb_o[:]))

                block.sync(a_dbg2)
            a_pv_marks.append(cnt["VE"])
            a_o_marks.extend([cnt["VE"]] * 2)
            a_x_done.append(cnt["VE"])
            ve_x1 = cnt["VE"]

            def a_x1st(sync, st=st, ve_x1=ve_x1):
                wait(sync, "VE", ve_x1)
                a_x1_store.append(sx1s.issue(
                    sync.dma_start(d_x1[st * 128:(st + 1) * 128, :], sb_x1[:])))

            block.sync(a_x1st)

            def _ln2_wait(scalar, ve_x1=ve_x1):
                wait(scalar, "VE", ve_x1)

            ac_rstd2 = ln_emit(sb_x1[:], sb_scB[:], ac_wait=_ln2_wait)

            def a_apply2(vector, ac=ac_rstd2):
                wait(vector, "AC", ac)
                x1v = sb_x1[:].rearrange("p (c two) -> p c two", two=2)
                vector.tensor_scalar(out=sb_xln[:, 0:2048],
                                     in0=x1v[:, :, 0:1],
                                     scalar1=sb_sm[:, 2:3], scalar2=sb_sm[:, 5:6],
                                     op0=ALU.subtract, op1=ALU.mult)
                vector.tensor_scalar(out=sb_xln[:, 2048:4096],
                                     in0=x1v[:, :, 1:2],
                                     scalar1=sb_sm[:, 2:3], scalar2=sb_sm[:, 5:6],
                                     op0=ALU.subtract, op1=ALU.mult)
                vector.reduce_sum(sb_xg[:],
                                  sb_xln[:, 0:2048].rearrange(
                                      "p (g c) -> p g c", g=64), axis=AX.X)
                vector.reduce_sum(sb_scB[:, 0:64],
                                  sb_xln[:, 2048:4096].rearrange(
                                      "p (g c) -> p g c", g=64), axis=AX.X)
                vector.drain()
                ii = vector.tensor_tensor(out=sb_xgb[:], in0=sb_xg[:],
                                          in1=sb_scB[:, 0:64], op=ALU.add)
                incd(vector, "VE", 1)

            block.vector(a_apply2)
            ve_xln2 = cnt["VE"]

            a_tr2_evac = []
            for w in range(4):
                pcol = 512 + (w % 2) * 1024

                def a_tr2(tensor, w=w, pcol=pcol, ve_xln2=ve_xln2):
                    if w == 0:
                        wait(tensor, "VE", ve_xln2)
                    if w < 2 and a_prev_set[0] is not None:
                        nm_, mk_ = a_prev_set[0][w + 2]
                        wait(tensor, nm_, mk_)
                    if w >= 2:
                        nm_, mk_ = a_tr2_evac[w - 2]
                        wait(tensor, nm_, mk_)
                    ii = None
                    for j in range(8):
                        kt = w * 8 + j
                        ii = tensor.matmul(
                            psum[:, pcol + j * 128:pcol + (j + 1) * 128],
                            lhsT=sb_xln[:, kt * 128:(kt + 1) * 128],
                            rhs=sb_id[:], start=True, stop=True)
                    inc(ii, "PE", 1)

                block.tensor(a_tr2)
                pm = cnt["PE"]

                if w % 2 == 0:
                    def a_tr2_ev(vector, w=w, pcol=pcol, pm=pm, st=st):
                        wait(vector, "PE", pm)
                        dst = sb_x2t[:].rearrange(
                            "p (kt s) -> p kt s", s=S_TOK)[
                            :, w * 8:(w + 1) * 8, st * 128:(st + 1) * 128]
                        vector.tensor_copy(
                            dst, psum[:, pcol:pcol + 1024].rearrange(
                                "p (kt s) -> p kt s", s=128))
                        incd(vector, "VE", 1)

                    block.vector(a_tr2_ev)
                    a_tr2_evac.append(("VE", cnt["VE"]))
                else:
                    def a_tr2_ev(scalar, w=w, pcol=pcol, pm=pm, st=st):
                        wait(scalar, "PE", pm)
                        dst = sb_x2t[:].rearrange(
                            "p (kt s) -> p kt s", s=S_TOK)[
                            :, w * 8:(w + 1) * 8, st * 128:(st + 1) * 128]
                        scalar.copy(
                            dst, psum[:, pcol:pcol + 1024].rearrange(
                                "p (kt s) -> p kt s", s=128))
                        incd(scalar, "AC", 1)

                    block.scalar(a_tr2_ev)
                    a_tr2_evac.append(("AC", cnt["AC"]))

            a_prev_set[0] = a_tr2_evac
            ve_trdone = cnt["VE"]
            ac_trdone = max(mk for nm, mk in a_tr2_evac if nm == "AC")

            def a_xg_tr(tensor, ve_trdone=ve_trdone, ac_trdone=ac_trdone):
                wait(tensor, "VE", ve_trdone)
                wait(tensor, "AC", ac_trdone)
                ii = tensor.matmul(psum[0:64, 2560:2688], lhsT=sb_xgb[:],
                                   rhs=sb_id[:], start=True, stop=True)
                incd(tensor, "PE", 1)

            block.tensor(a_xg_tr)
            pm = cnt["PE"]

            def a_xg_ev(vector, st=st, pm=pm):
                wait(vector, "PE", pm)
                vector.tensor_copy(sb_xgT[0:64, st * 128:(st + 1) * 128],
                                   psum[0:64, 2560:2688])
                ii = vector.tensor_copy(sb_xgT[64:128, st * 128:(st + 1) * 128],
                                        psum[0:64, 2560:2688])
                incd(vector, "VE", 1)

            block.vector(a_xg_ev)

        ve_A = cnt["VE"]
        pe_A = cnt["PE"]
        pa.close()

        pb = contextlib.ExitStack()
        sb_scg = pb.enter_context(nc.sbuf_tensor("sb_scg", [128, TI * 64], BF16))
        sb_scu = pb.enter_context(nc.sbuf_tensor("sb_scu", [128, TI * 64], BF16))
        sb_zsT = pb.enter_context(nc.sbuf_tensor("sb_zsT", [128, TI * 128], BF16))

        pbp = contextlib.ExitStack()
        sb_sz = pbp.enter_context(nc.sbuf_tensor("sb_sz", [128, 2 * TI * 64], F32))
        sb_zs = pbp.enter_context(nc.sbuf_tensor("sb_zs", [128, TI * 64], BF16))
        sb_zs2 = pbp.enter_context(nc.sbuf_tensor("sb_zs2", [128, TI * 64], F32))

        ac_Aend = cnt["AC"]
        for wi, (dsc, dzr, scbf, roff) in enumerate(
                [(d_scg, d_zrg, sb_scg, 0), (d_scu, d_zru, sb_scu, 64)]):
            sz_tok = [None]

            def bp_load(sync, dsc=dsc, dzr=dzr, wi=wi, sz_tok=sz_tok):
                if wi == 0:
                    wait(sync, "VE", ve_A)
                    sx1s.wait_all(sync)
                else:
                    wait(sync, "VE", bp_ve_marks[wi - 1])
                sax.issue(sync.dma_start(sb_sz[:, 0:TI * 64], dsc[:]),
                          gslot=1 + wi)
                sz_tok[0] = sax.issue(
                    sync.dma_start(sb_sz[:, TI * 64:2 * TI * 64], dzr[:]),
                    gslot=1 + wi)

            block.sync(bp_load)

            def bp_ve(vector, scbf=scbf, sz_tok=sz_tok):
                sax.wait(vector, sz_tok[0])
                vector.tensor_copy(scbf[:], sb_sz[:, 0:TI * 64])
                vector.tensor_scalar(out=sb_zs2[:],
                                     in0=sb_sz[:, TI * 64:2 * TI * 64],
                                     scalar1=7.5, scalar2=-1.0,
                                     op0=ALU.subtract, op1=ALU.mult)
                vector.drain()
                ii = vector.tensor_tensor(out=sb_zs[:], in0=sb_zs2[:],
                                          in1=sb_sz[:, 0:TI * 64], op=ALU.mult)
                incd(vector, "VE", 1)

            block.vector(bp_ve)
            bp_ve_marks = bp_ve_marks if wi else []
            bp_ve_marks.append(cnt["VE"])
            ve_zs = cnt["VE"]

            bp_evacs = []
            for w0 in range(0, TI, 8):
                nw = min(8, TI - w0)
                widx = w0 // 8
                pcol = (widx % 2) * 1024

                def bp_tr(tensor, w0=w0, nw=nw, widx=widx, pcol=pcol,
                          ve_zs=ve_zs, wi=wi):
                    if widx == 0:
                        wait(tensor, "VE", ve_zs)
                        if wi == 0:
                            wait(tensor, "AC", ac_Aend)
                    if widx >= 2:
                        wait(tensor, "VE", bp_evacs[widx - 2])
                    ii = None
                    for j in range(nw):
                        t = w0 + j
                        ii = tensor.matmul(
                            psum[0:64, pcol + j * 128:pcol + (j + 1) * 128],
                            lhsT=sb_zs[:, t * 64:(t + 1) * 64], rhs=sb_id[:],
                            start=True, stop=True)
                    incd(tensor, "PE", 1)

                block.tensor(bp_tr)
                pm = cnt["PE"]

                def bp_ev(vector, w0=w0, nw=nw, pcol=pcol, pm=pm, roff=roff):
                    wait(vector, "PE", pm)
                    ii = vector.tensor_copy(
                        sb_zsT[roff:roff + 64, w0 * 128:(w0 + nw) * 128],
                        psum[0:64, pcol:pcol + nw * 128])
                    incd(vector, "VE", 1)

                block.vector(bp_ev)
                bp_evacs.append(cnt["VE"])

        pbp.close()
        ve_bp = cnt["VE"]
        pe_bp = cnt["PE"]

        sb_pk = pb.enter_context(nc.sbuf_tensor("sb_pk", [128, 4 * PACK_GU], I16))
        sb_hi = pb.enter_context(nc.sbuf_tensor("sb_hi", [128, PACK_GU], I16))
        sb_lo = pb.enter_context(nc.sbuf_tensor("sb_lo", [128, PACK_GU], I16))
        sb_wn = pb.enter_context(nc.sbuf_tensor("sb_wn", [128, 4 * D], BF16))
        sb_wt = pb.enter_context(nc.sbuf_tensor("sb_wt", [128, 4 * D], BF16))
        sb_silu = pb.enter_context(nc.sbuf_tensor("sb_silu", [128, 1024], F32))
        sb_actb = pb.enter_context(nc.sbuf_tensor("sb_actb", [128, 2 * 1024], BF16))

        b_deq_done = []
        b_tr_done = {}
        b_act_done = []
        b_act_store = []
        b_silu_marks = []
        b_wave_marks = {}
        b_pk_toks = []

        def b_emit_load(t):
            pk_tok = [None]

            def b_load(gpsimd, t=t, pk_tok=pk_tok):
                if t == 0:
                    wait(gpsimd, "VE", ve_bp)
                    sx1s.wait_all(gpsimd)
                if t >= 2:
                    wait(gpsimd, "VE", b_deq_done[t - 2])
                o = (t % 2) * PACK_GU
                sgp.issue(gpsimd.dma_start(
                    sb_pk[:, o:o + PACK_GU],
                    d_gp[:, t * PACK_GU:(t + 1) * PACK_GU]), gslot=t)
                pk_tok[0] = sgp.issue(gpsimd.dma_start(
                    sb_pk[:, 2 * PACK_GU + o:2 * PACK_GU + o + PACK_GU],
                    d_up[:, t * PACK_GU:(t + 1) * PACK_GU]), gslot=t)

            block.gpsimd(b_load)
            b_pk_toks.append(pk_tok)

        def b_emit_deq(t):
            def b_deq(vector, t=t):
                sgp.wait(vector, b_pk_toks[t][0])
                if t >= 2:
                    wait(vector, "PE", b_tr_done[t - 2])
                wb = (t % 2) * 2 * D
                for which in range(2):
                    dsc = sb_scg if which == 0 else sb_scu
                    o = (t % 2) * PACK_GU + which * 2 * PACK_GU
                    pk = sb_pk[:, o:o + PACK_GU]
                    vector.tensor_scalar(out=sb_hi[:], in0=pk, scalar1=4,
                                         scalar2=None,
                                         op0=ALU.logical_shift_right)
                    vector.tensor_scalar(out=sb_lo[:], in0=pk, scalar1=15,
                                         scalar2=None, op0=ALU.bitwise_and)
                    s_bc = dsc[:, t * 64:(t + 1) * 64][:, :, None].broadcast_to(
                        [128, 64, 32])
                    wv = sb_wn[:, wb + which * D:wb + which * D + D]
                    vector.scalar_tensor_tensor(
                        out=wv[:, 0:2048].rearrange("p (g c) -> p g c", g=64),
                        in0=sb_hi[:].rearrange("p (g c) -> p g c", g=64),
                        scalar=-7.5, op0=ALU.add, op1=ALU.mult, in1=s_bc)
                    vector.scalar_tensor_tensor(
                        out=wv[:, 2048:4096].rearrange("p (g c) -> p g c", g=64),
                        in0=sb_lo[:].rearrange("p (g c) -> p g c", g=64),
                        scalar=-7.5, op0=ALU.add, op1=ALU.mult, in1=s_bc)
                incd(vector, "VE", 1)

            block.vector(b_deq)
            b_deq_done.append(cnt["VE"])

        def b_emit_actmul(t):
            def b_actmul(vector, t=t):
                wait(vector, "AC", b_silu_marks[t])
                if t >= 2:
                    sas.wait(vector, b_act_store[t - 2])
                ab = sb_actb[:, (t % 2) * 1024:(t % 2) * 1024 + 1024]
                vector.tensor_tensor(out=ab, in0=sb_silu[:],
                                     in1=psum[:, 1024:2048], op=ALU.mult)
                incd(vector, "VE", 1)

            block.vector(b_actmul)
            b_act_done.append(cnt["VE"])
            vm = cnt["VE"]

            def b_actst(sync, t=t, vm=vm):
                wait(sync, "VE", vm)
                ab = sb_actb[:, (t % 2) * 1024:(t % 2) * 1024 + 1024]
                b_act_store.append(sas.issue(
                    sync.dma_start(d_act[t * 128:(t + 1) * 128, :], ab)))

            block.sync(b_actst)

        def b_emit_wave(tw, w):
            pcol = 2048 + (w % 2) * 1024
            wb = (tw % 2) * 2 * D
            ve_deq = b_deq_done[tw]

            def b_tr(tensor, tw=tw, w=w, pcol=pcol, ve_deq=ve_deq, wb=wb):
                if w == 0:
                    wait(tensor, "VE", ve_deq)
                    if tw == 0:
                        wait(tensor, "AC", ac_A)
                    if tw >= 1:
                        wait(tensor, "VE", b_wave_marks[tw - 1][6])
                if w == 1 and tw >= 1:
                    wait(tensor, "VE", b_wave_marks[tw - 1][7])
                if w >= 2:
                    wait(tensor, "VE", b_wave_marks[tw][w - 2])
                ii = None
                for j in range(8):
                    g = w * 8 + j
                    which, kt = divmod(g, 32)
                    ii = tensor.matmul(
                        psum[:, pcol + j * 128:pcol + (j + 1) * 128],
                        lhsT=sb_wn[:, wb + which * D + kt * 128:
                                   wb + which * D + (kt + 1) * 128],
                        rhs=sb_id[:], start=True, stop=True)
                inc(ii, "PE", 1)

            block.tensor(b_tr)
            pm = cnt["PE"]
            wtb = (tw % 2) * 2 * D

            def b_tr_ev(vector, w=w, pcol=pcol, pm=pm, wtb=wtb):
                wait(vector, "PE", pm)
                vector.tensor_copy(sb_wt[:, wtb + w * 1024:wtb + (w + 1) * 1024],
                                   psum[:, pcol:pcol + 1024])
                incd(vector, "VE", 1)

            block.vector(b_tr_ev)
            b_wave_marks.setdefault(tw, {})[w] = cnt["VE"]
            if w == 7:
                b_tr_done[tw] = pm

        def b_emit_mm(t, with_waves_of):
            wtb = (t % 2) * 2 * D
            jobs = []
            for which in range(2):
                roff = 0 if which == 0 else 64
                for half in range(2):
                    pc = which * 1024 + half * 512
                    for kt in range(DK):
                        jobs.append((pc, which * 32 + kt, half, kt == 0, False,
                                     None))
                    jobs.append((pc, None, half, False, True, roff))
            segs = [jobs[i * 17:(i + 1) * 17] for i in range(8)]
            ve_wt = max(b_wave_marks[t].values())

            for w in range(8):
                if with_waves_of is not None:
                    b_emit_wave(with_waves_of, w)

                def b_mm_seg(tensor, t=t, w=w, seg=segs[w], ve_wt=ve_wt,
                             wtb=wtb):
                    if w == 0:
                        wait(tensor, "VE", ve_wt)
                        if t > 0:
                            wait(tensor, "VE", b_act_done[t - 1])
                    ii = None
                    for pc, sl, half, st_, sp_, roff in seg:
                        if sl is not None:
                            ii = tensor.matmul(
                                psum[:, pc:pc + 512],
                                lhsT=sb_wt[:, wtb + sl * 128:wtb + (sl + 1) * 128],
                                rhs=sb_x2t[:, (sl % 32) * S_TOK + half * 512:
                                           (sl % 32) * S_TOK + half * 512 + 512],
                                start=st_, stop=False)
                        else:
                            ii = tensor.matmul(
                                psum[:, pc:pc + 512],
                                lhsT=sb_zsT[roff:roff + 64,
                                            t * 128:(t + 1) * 128],
                                rhs=sb_xgT[roff:roff + 64,
                                           half * 512:half * 512 + 512],
                                start=False, stop=True)
                    if w == 7:
                        incd(tensor, "PE", 1)
                    else:
                        inc(ii, "PE", 1)

                block.tensor(b_mm_seg)

            pm = cnt["PE"]

            def b_silu(scalar, pm=pm):
                wait(scalar, "PE", pm)
                scalar.activation(sb_silu[:], psum[:, 0:1024], AF.Silu)
                incd(scalar, "AC", 1)

            block.scalar(b_silu)
            b_silu_marks.append(cnt["AC"])

        ac_A = cnt["AC"]
        b_emit_load(0)
        b_emit_deq(0)
        b_emit_load(1)
        b_emit_deq(1)
        for w in range(8):
            b_emit_wave(0, w)
        for t in range(TI):
            if t >= 1:
                b_emit_actmul(t - 1)
            if t + 2 < TI:
                b_emit_load(t + 2)
                b_emit_deq(t + 2)
            b_emit_mm(t, t + 1 if t + 1 < TI else None)
        b_emit_actmul(TI - 1)

        pe_B = cnt["PE"]
        ve_B = cnt["VE"]
        pb.close()

        pco = contextlib.ExitStack()
        sb_zsdT1 = pco.enter_context(
            nc.sbuf_tensor("sb_zsdT1", [128, HT * 128], BF16))
        sb_zsdT2 = pco.enter_context(
            nc.sbuf_tensor("sb_zsdT2", [44, HT * 128], BF16))
        sb_actt = pco.enter_context(
            nc.sbuf_tensor("sb_actt", [128, 8 * 1024], BF16))
        sb_ag1 = pco.enter_context(nc.sbuf_tensor("sb_ag1", [128, S_TOK], BF16))
        sb_ag2 = pco.enter_context(nc.sbuf_tensor("sb_ag2", [44, S_TOK], BF16))
        pb2 = contextlib.ExitStack()
        sb_scd = pb2.enter_context(nc.sbuf_tensor("sb_scd", [128, HT * 172], BF16))
        pb2p = contextlib.ExitStack()
        sb_szd = pb2p.enter_context(nc.sbuf_tensor("sb_szd", [128, 2 * HT * 172], F32))
        sb_zsd = pb2p.enter_context(nc.sbuf_tensor("sb_zsd", [128, HT * 172], BF16))
        sb_zsd2 = pb2p.enter_context(nc.sbuf_tensor("sb_zsd2", [128, HT * 172], F32))

        szd_tok = [None]

        def b2p_load(sync):
            wait(sync, "VE", ve_B)
            sas.wait_all(sync)
            sax.issue(sync.dma_start(sb_szd[:, 0:HT * 172], d_scd[:]), gslot=3)
            szd_tok[0] = sax.issue(
                sync.dma_start(sb_szd[:, HT * 172:2 * HT * 172], d_zrd[:]),
                gslot=3)

        block.sync(b2p_load)

        def b2p_ve(vector):
            sax.wait(vector, szd_tok[0])
            vector.tensor_copy(sb_scd[:], sb_szd[:, 0:HT * 172])
            vector.tensor_scalar(out=sb_zsd2[:],
                                 in0=sb_szd[:, HT * 172:2 * HT * 172],
                                 scalar1=7.5, scalar2=-1.0,
                                 op0=ALU.subtract, op1=ALU.mult)
            vector.drain()
            ii = vector.tensor_tensor(out=sb_zsd[:], in0=sb_zsd2[:],
                                      in1=sb_szd[:, 0:HT * 172], op=ALU.mult)
            incd(vector, "VE", 1)

        block.vector(b2p_ve)
        ve_zsd = cnt["VE"]

        b2p_marks = []
        for ht in range(HT):
            def b2p_tr(tensor, ht=ht, ve_zsd=ve_zsd):
                if ht == 0:
                    wait(tensor, "VE", ve_zsd)
                if ht >= 2:
                    wait(tensor, "VE", b2p_marks[ht - 2])
                pcol = (ht % 2) * 512
                tensor.matmul(psum[0:128, pcol:pcol + 128],
                              lhsT=sb_zsd[:, ht * 172:ht * 172 + 128],
                              rhs=sb_id[:], start=True, stop=True)
                ii = tensor.matmul(psum[0:44, pcol + 128:pcol + 256],
                                   lhsT=sb_zsd[:, ht * 172 + 128:ht * 172 + 172],
                                   rhs=sb_id[:], start=True, stop=True)
                incd(tensor, "PE", 1)

            block.tensor(b2p_tr)
            pm = cnt["PE"]

            def b2p_ev(vector, ht=ht, pm=pm):
                wait(vector, "PE", pm)
                pcol = (ht % 2) * 512
                vector.tensor_copy(sb_zsdT1[:, ht * 128:(ht + 1) * 128],
                                   psum[0:128, pcol:pcol + 128])
                ii = vector.tensor_copy(sb_zsdT2[0:44, ht * 128:(ht + 1) * 128],
                                        psum[0:44, pcol + 128:pcol + 256])
                incd(vector, "VE", 1)

            block.vector(b2p_ev)
            b2p_marks.append(cnt["VE"])

        pb2p.close()
        ve_b2p = cnt["VE"]

        sb_pkd = pb2.enter_context(nc.sbuf_tensor("sb_pkd", [128, 2 * PACK_D], I16))
        sb_hid = pb2.enter_context(nc.sbuf_tensor("sb_hid", [128, PACK_D], I16))
        sb_lod = pb2.enter_context(nc.sbuf_tensor("sb_lod", [128, PACK_D], I16))
        sb_wnd = pb2.enter_context(nc.sbuf_tensor("sb_wnd", [128, INTER], BF16))
        sb_wtd = pb2.enter_context(nc.sbuf_tensor("sb_wtd", [128, TI * 128], BF16))

        b2_deq_done = []
        b2_tr_done = []
        b2_store_done = []

        c0_pe_marks = []
        act_toks = {}

        def emit_c0(t):
            def c0_load(sync, t=t):
                sas.wait(sync, b_act_store[t])
                if t >= 8:
                    wait(sync, "PE", c0_pe_marks[t - 8])
                o = (t % 8) * 1024
                act_toks[t] = sal.issue(
                    sync.dma_start(sb_actt[:, o:o + 1024],
                                   d_act[t * 128:(t + 1) * 128, :]))

            block.sync(c0_load)

            def c0_mm(tensor, t=t):
                sal.wait(tensor, act_toks[t])
                if t == 0:
                    wait(tensor, "VE", ve_B)
                o = (t % 8) * 1024
                if t < 32:
                    base, T, first, last = 2048, t, (t == 0), False
                elif t < 43:
                    base, T, first, last = 3072, t - 32, (t == 32), False
                elif t < 75:
                    base, T, first, last = 2048, t - 43, False, (t == 74)
                else:
                    base, T, first, last = 3072, t - 75, False, (t == 85)
                lhs = sb_bigo[:, 124 - 4 * T:252 - 4 * T]
                ii = None
                for half in range(2):
                    ii = tensor.matmul(
                        psum[:, base + half * 512:base + half * 512 + 512],
                        lhsT=lhs,
                        rhs=sb_actt[:, o + half * 512:o + half * 512 + 512],
                        start=first, stop=last)
                inc(ii, "PE", 1)

            block.tensor(c0_mm)
            c0_pe_marks.append(cnt["PE"])

        for ht in range(HT):
            pkd_tok = [None]

            def b2_load(gpsimd, ht=ht, pkd_tok=pkd_tok):
                if ht == 0:
                    wait(gpsimd, "VE", ve_b2p)
                    sas.wait_all(gpsimd)
                if ht >= 2:
                    wait(gpsimd, "VE", b2_deq_done[ht - 2])
                o = (ht % 2) * PACK_D
                pkd_tok[0] = sdp.issue(gpsimd.dma_start(
                    sb_pkd[:, o:o + PACK_D],
                    d_dp[:, ht * PACK_D:(ht + 1) * PACK_D]))

            block.gpsimd(b2_load)

            def b2_deq(vector, ht=ht, pkd_tok=pkd_tok):
                sdp.wait(vector, pkd_tok[0])
                if ht > 0:
                    wait(vector, "PE", b2_tr_done[ht - 1])
                o = (ht % 2) * PACK_D
                pk = sb_pkd[:, o:o + PACK_D]
                vector.tensor_scalar(out=sb_hid[:], in0=pk, scalar1=4,
                                     scalar2=None, op0=ALU.logical_shift_right)
                vector.tensor_scalar(out=sb_lod[:], in0=pk, scalar1=15,
                                     scalar2=None, op0=ALU.bitwise_and)
                s_bc = sb_scd[:, ht * 172:(ht + 1) * 172][:, :, None] \
                    .broadcast_to([128, 172, 32])
                vector.scalar_tensor_tensor(
                    out=sb_wnd[:, 0:PACK_D].rearrange("p (g c) -> p g c", g=172),
                    in0=sb_hid[:].rearrange("p (g c) -> p g c", g=172),
                    scalar=-7.5, op0=ALU.add, op1=ALU.mult, in1=s_bc)
                ii = vector.scalar_tensor_tensor(
                    out=sb_wnd[:, PACK_D:INTER].rearrange(
                        "p (g c) -> p g c", g=172),
                    in0=sb_lod[:].rearrange("p (g c) -> p g c", g=172),
                    scalar=-7.5, op0=ALU.add, op1=ALU.mult, in1=s_bc)
                incd(vector, "VE", 1)

            block.vector(b2_deq)
            b2_deq_done.append(cnt["VE"])
            ve_deq = cnt["VE"]

            b2_evacs = []
            for w0 in range(0, TI, 8):
                nw = min(8, TI - w0)
                widx = w0 // 8
                pcol = (widx % 2) * 1024

                def b2_tr(tensor, ht=ht, w0=w0, nw=nw, widx=widx, pcol=pcol,
                          ve_deq=ve_deq):
                    if widx == 0:
                        wait(tensor, "VE", ve_deq)
                        if ht >= 1:
                            sws.wait(tensor, b2_store_done[ht - 1])
                    if widx >= 2:
                        nm_, mk_ = b2_evacs[widx - 2]
                        wait(tensor, nm_, mk_)
                    ii = None
                    for j in range(nw):
                        kt = w0 + j
                        ii = tensor.matmul(
                            psum[:, pcol + j * 128:pcol + (j + 1) * 128],
                            lhsT=sb_wnd[:, kt * 128:(kt + 1) * 128],
                            rhs=sb_id[:], start=True, stop=True)
                    inc(ii, "PE", 1)

                block.tensor(b2_tr)
                pm = cnt["PE"]

                if widx % 2 == 0:
                    def b2_tr_ev(vector, w0=w0, nw=nw, pcol=pcol, pm=pm):
                        wait(vector, "PE", pm)
                        vector.tensor_copy(
                            sb_wtd[:, w0 * 128:(w0 + nw) * 128],
                            psum[:, pcol:pcol + nw * 128])
                        incd(vector, "VE", 1)

                    block.vector(b2_tr_ev)
                    b2_evacs.append(("VE", cnt["VE"]))
                else:
                    def b2_tr_ev(scalar, w0=w0, nw=nw, pcol=pcol, pm=pm):
                        wait(scalar, "PE", pm)
                        scalar.copy(
                            sb_wtd[:, w0 * 128:(w0 + nw) * 128],
                            psum[:, pcol:pcol + nw * 128])
                        incd(scalar, "AC", 1)

                    block.scalar(b2_tr_ev)
                    b2_evacs.append(("AC", cnt["AC"]))

            b2_tr_done.append(cnt["PE"])
            vm = cnt["VE"]
            am2 = cnt["AC"]

            def b2_st(sync, ht=ht, vm=vm, am2=am2):
                wait(sync, "VE", vm)
                wait(sync, "AC", am2)
                dst = d_wdt[:].rearrange("p (kt ht h) -> p kt ht h",
                                         kt=TI, ht=HT)[:, :, ht, :]
                b2_store_done.append(sws.issue(sync.dma_start(
                    dst, sb_wtd[:].rearrange("p (kt h) -> p kt h", kt=TI))))

            block.sync(b2_st)

            for t in range((TI * ht) // HT, (TI * (ht + 1)) // HT):
                emit_c0(t)

        pe_B2 = cnt["PE"]
        ve_B2 = cnt["VE"]
        pb2.close()

        pc_ = contextlib.ExitStack()

        def c0_fence(tensor):
            incd(tensor, "PE", 1)

        block.tensor(c0_fence)
        pe_c0 = cnt["PE"]

        def c0_ev(vector):
            wait(vector, "PE", pe_c0)
            vector.tensor_copy(sb_ag1[:], psum[:, 2048:3072])
            ii = vector.tensor_copy(sb_ag2[0:44, :], psum[0:44, 3072:4096])
            incd(vector, "VE", 1)

        block.vector(c0_ev)
        ve_ag = cnt["VE"]

        if debug:
            def c0_dump(sync):
                wait(sync, "VE", ve_ag)
                sdbg.issue(sync.dma_start(d_agd[0:128, :], sb_ag1[:]))
                sdbg.issue(sync.dma_start(d_agd[128:172, :], sb_ag2[0:44, :]))

            block.sync(c0_dump)

        sb_wdtb = pc_.enter_context(
            nc.sbuf_tensor("sb_wdtb", [128, 2 * 16 * 512], BF16))
        sb_x1c = pc_.enter_context(
            nc.sbuf_tensor("sb_x1c", [128, 2 * 8 * 512], F32))
        sb_y = pc_.enter_context(nc.sbuf_tensor("sb_y", [128, 8 * 512], F32))

        KT_BATCH = [list(range(b, min(b + 16, TI))) for b in range(0, TI, 16)]
        wdt_v = d_wdt[:].rearrange("p (kt ht h) -> p kt ht h", kt=TI, ht=HT)

        c_evac_marks = []
        c_x1_tok = {}
        c_y_tok = {}
        c_wdt_used = []
        c_act_used = []
        for hc in range(NHC):
            def c_x1load(sync, hc=hc):
                if hc == 0:
                    wait(sync, "VE", ve_ag)
                    sx1s.wait_all(sync)
                if hc >= 2:
                    wait(sync, "VE", c_evac_marks[hc - 2])
                o = (hc % 2) * 8 * 512
                for st2 in range(NT):
                    c_x1_tok[hc] = sxl.issue(sync.dma_start(
                        sb_x1c[:, o + st2 * 512:o + (st2 + 1) * 512],
                        d_x1[st2 * 128:(st2 + 1) * 128,
                             hc * 512:(hc + 1) * 512]), gslot=hc)

            block.sync(c_x1load)

            for bi, batch in enumerate(KT_BATCH):
                wdt_tok = [None]

                def c_wdtload(sync, hc=hc, bi=bi, batch=batch):
                    if hc == 0 and bi == 0:
                        sws.wait_all(sync)
                        sdp.wait_all(sync)
                    if len(c_wdt_used) >= 2:
                        wait(sync, "PE", c_wdt_used[-2])
                    o = (bi % 2) * 16 * 512
                    nb = len(batch)
                    src = wdt_v[:, batch[0]:batch[0] + nb,
                                hc * 4:(hc + 1) * 4, :]
                    dst = sb_wdtb[:, o:o + 16 * 512].rearrange(
                        "p (kt ht h) -> p kt ht h", kt=16, ht=4)[:, 0:nb, :, :]
                    wdt_tok[0] = swl.issue(sync.dma_start(dst, src))

                block.sync(c_wdtload)

                for kk, kt in enumerate(batch):
                    gidx = hc * TI + kt

                    ca_tok = [None]

                    def c_actload(sync, hc=hc, kt=kt, gidx=gidx, ca_tok=ca_tok):
                        if gidx < 8:
                            wait(sync, "PE", pe_c0)
                        else:
                            wait(sync, "PE", c_act_used[gidx - 8])
                        o = (gidx % 8) * 1024
                        ca_tok[0] = sal.issue(
                            sync.dma_start(sb_actt[:, o:o + 1024],
                                           d_act[kt * 128:(kt + 1) * 128, :]))

                    block.sync(c_actload)

                    def c_mm(tensor, hc=hc, bi=bi, kk=kk, kt=kt,
                             wdt_tok=wdt_tok, ca_tok=ca_tok):
                        swl.wait(tensor, wdt_tok[0])
                        sal.wait(tensor, ca_tok[0])
                        if hc == 0 and bi == 0 and kk == 0:
                            wait(tensor, "VE", ve_ag)
                        if hc > 0 and bi == 0 and kk == 0:
                            wait(tensor, "VE", c_evac_marks[hc - 1])
                        o = ((hc * TI + kt) % 8) * 1024
                        ob = (bi % 2) * 16 * 512
                        rhs = sb_wdtb[:, ob + kk * 512:ob + (kk + 1) * 512]
                        ii = None
                        for st2 in range(NT):
                            ii = tensor.matmul(
                                psum[:, st2 * 512:(st2 + 1) * 512],
                                lhsT=sb_actt[:, o + st2 * 128:
                                             o + (st2 + 1) * 128],
                                rhs=rhs,
                                start=(bi == 0 and kk == 0), stop=False)
                        inc(ii, "PE", 1)

                    block.tensor(c_mm)
                    c_act_used.append(cnt["PE"])

                c_wdt_used.append(cnt["PE"])

            def c_corr(tensor, hc=hc):
                ii = None
                for st2 in range(NT):
                    tensor.matmul(
                        psum[:, st2 * 512:(st2 + 1) * 512],
                        lhsT=sb_ag1[:, st2 * 128:(st2 + 1) * 128],
                        rhs=sb_zsdT1[:, hc * 512:(hc + 1) * 512],
                        start=False, stop=False)
                    ii = tensor.matmul(
                        psum[:, st2 * 512:(st2 + 1) * 512],
                        lhsT=sb_ag2[0:44, st2 * 128:(st2 + 1) * 128],
                        rhs=sb_zsdT2[0:44, hc * 512:(hc + 1) * 512],
                        start=False, stop=True)
                incd(tensor, "PE", 1)

            block.tensor(c_corr)
            pe_hc = cnt["PE"]

            def c_evac(vector, hc=hc, pe_hc=pe_hc):
                wait(vector, "PE", pe_hc)
                sxl.wait(vector, c_x1_tok[hc])
                if hc >= 1:
                    sy.wait(vector, c_y_tok[hc - 1])
                o = (hc % 2) * 8 * 512
                ii = None
                for st2 in range(NT):
                    ii = vector.tensor_tensor(
                        out=sb_y[:, st2 * 512:(st2 + 1) * 512],
                        in0=psum[:, st2 * 512:(st2 + 1) * 512],
                        in1=sb_x1c[:, o + st2 * 512:o + (st2 + 1) * 512],
                        op=ALU.add)
                incd(vector, "VE", 1)

            block.vector(c_evac)
            c_evac_marks.append(cnt["VE"])
            vm = cnt["VE"]

            def c_store(sync, hc=hc, vm=vm):
                wait(sync, "VE", vm)
                for st2 in range(NT):
                    c_y_tok[hc] = sy.issue(sync.dma_start(
                        d_y[st2 * 128:(st2 + 1) * 128,
                            hc * 512:(hc + 1) * 512],
                        sb_y[:, st2 * 512:(st2 + 1) * 512]), gslot=hc)

            block.sync(c_store)

        def fin(sync):
            for s_ in ALL_STREAMS:
                s_.wait_all(sync)

        block.sync(fin)
        pc_.close()
        pco.close()

    return nc


def kernel(L0_attn_x, L0_attn_Wq, L0_attn_bq, L0_attn_Wk, L0_attn_bk,
           L0_attn_Wv, L0_attn_bv, L0_ffn_gate_Wq, L0_ffn_up_Wq,
           L0_ffn_down_Wq, L0_ffn_gate_scale, L0_ffn_gate_zero,
           L0_ffn_up_scale, L0_ffn_up_zero, L0_ffn_down_scale,
           L0_ffn_down_zero):
    bf = ml_dtypes.bfloat16
    x = np.ascontiguousarray(np.asarray(L0_attn_x, dtype=np.float32)
                             .reshape(8192, 4096))
    perm = np.concatenate([np.arange(0, INTER, 2), np.arange(1, INTER, 2)])

    def tile_rows(a, ntile):
        a = np.ascontiguousarray(a)
        r, c = a.shape
        assert r == ntile * 128
        return np.ascontiguousarray(
            a.reshape(ntile, 128, c).transpose(1, 0, 2).reshape(128, ntile * c))

    gate_p = tile_rows(np.asarray(L0_ffn_gate_Wq, dtype=np.int32)[perm], TI)
    up_p = tile_rows(np.asarray(L0_ffn_up_Wq, dtype=np.int32)[perm], TI)
    down_p = tile_rows(np.asarray(L0_ffn_down_Wq, dtype=np.int32), HT)
    sc_g = tile_rows(np.asarray(L0_ffn_gate_scale, dtype=np.float32)[perm], TI)
    zr_g = tile_rows(np.asarray(L0_ffn_gate_zero, dtype=np.float32)[perm], TI)
    sc_u = tile_rows(np.asarray(L0_ffn_up_scale, dtype=np.float32)[perm], TI)
    zr_u = tile_rows(np.asarray(L0_ffn_up_zero, dtype=np.float32)[perm], TI)
    sc_d = tile_rows(np.asarray(L0_ffn_down_scale, dtype=np.float32), HT)
    zr_d = tile_rows(np.asarray(L0_ffn_down_zero, dtype=np.float32), HT)

    wq = np.asarray(L0_attn_Wq, dtype=np.float32).reshape(4096, 4096)
    wq_t = tile_rows(wq, DK)
    wk_t = tile_rows(np.asarray(L0_attn_Wk, dtype=np.float32)[:, :, 0], DK)
    wv_t = tile_rows(np.asarray(L0_attn_Wv, dtype=np.float32)[:, :, 0], DK)
    bq = np.ascontiguousarray(np.asarray(L0_attn_bq, dtype=np.float32))
    bk = np.ascontiguousarray(np.asarray(L0_attn_bk, dtype=np.float32))
    bv = np.ascontiguousarray(np.asarray(L0_attn_bv, dtype=np.float32))

    ident = np.eye(128, dtype=bf)
    onesr = np.ones((1, 128), dtype=bf)
    bigo = np.zeros((128, 256), dtype=np.float32)
    for p in range(128):
        bigo[p, 124 + p // 32] = 1.0
    bigo = np.ascontiguousarray(bigo.astype(bf))

    shared = {
        "wq_t": wq_t, "wk_t": wk_t, "wv_t": wv_t, "bq": bq, "bk": bk, "bv": bv,
        "gate_p": gate_p, "up_p": up_p, "down_p": down_p,
        "sc_g": sc_g, "zr_g": zr_g, "sc_u": sc_u, "zr_u": zr_u,
        "sc_d": sc_d, "zr_d": zr_d,
        "ident": ident, "onesr": onesr, "bigo": bigo,
    }
    in_maps = []
    for c in range(N_CORES):
        m = dict(shared)
        m["x"] = np.ascontiguousarray(x[c * S_TOK:(c + 1) * S_TOK])
        in_maps.append(m)

    nc = build_nc()
    res = run_bass_kernel_spmd(nc, in_maps, core_ids=list(range(N_CORES)))
    y = np.concatenate([np.asarray(r["y"]) for r in res.results], axis=0)
    return np.ascontiguousarray(y.reshape(1, 8192, 4096))
